# version 22
# baseline (speedup 1.0000x reference)
import numpy as np
import ml_dtypes
from contextlib import ExitStack
import concourse.bass as bass
import concourse.mybir as mybir
from concourse.bass_utils import run_bass_kernel_spmd

F32 = mybir.dt.float32
BF16 = mybir.dt.bfloat16
I32 = mybir.dt.int32
I8 = mybir.dt.int8
AF = mybir.ActivationFunctionType
ALU = mybir.AluOpType
AX = mybir.AxisListType

D = 1024
S = 2048
TC = S // 128
FC = D // 128
NP = 8
QH = 1024
NQ = 4
KC = 16
EPS = 1e-5
QKS = float(1.0 / np.sqrt(np.sqrt(64.0)))


class Sem:
    def __init__(self, es, nc, name):
        self.h = es.enter_context(nc.semaphore(name))
        self.n = 0

    def inc(self, instr):
        self.n += 1
        return instr

    def dinc(self, instr):
        self.n += 16
        return instr


def _ap_tname(a):
    t = getattr(a, "tensor", None)
    return getattr(t, "name", None) if t is not None else None


class EngP:

    def __init__(self, eng, acc):
        self.__dict__["_e"] = eng
        self.__dict__["_acc"] = acc

    @property
    def raw(self):
        return self._e

    def drain(self):
        self._e.drain()
        self._acc.clear()

    def wait_ge(self, *a, **k):
        return self._e.wait_ge(*a, **k)

    def __getattr__(self, name):
        f = getattr(self._e, name)
        if not callable(f):
            return f
        acc = self._acc
        eng = self._e

        def wrap(*args, **kwargs):
            names = set()
            for a in list(args) + list(kwargs.values()):
                n = _ap_tname(a)
                if n is not None:
                    names.add(n)
            if names & acc:
                eng.drain()
                acc.clear()
            acc.update(names)
            return f(*args, **kwargs)
        return wrap


class BlkWrap:
    def __init__(self, blk, dirty):
        self._blk = blk
        self._dirty = dirty

    @property
    def vector(self):
        def deco(body):
            @self._blk.vector
            def _(v):
                body(EngP(v, self._dirty["v"]))
            return _
        return deco

    @property
    def scalar(self):
        def deco(body):
            @self._blk.scalar
            def _(s):
                body(EngP(s, self._dirty["s"]))
            return _
        return deco

    def __getattr__(self, name):
        return getattr(self._blk, name)


def build_nc():
    nc = bass.Bass("TRN2", target_bir_lowering=False)

    x_h = nc.dram_tensor("x", [S, D], F32, kind="ExternalInput").ap()
    g_h = nc.dram_tensor("ln_g", [1, D], F32, kind="ExternalInput").ap()
    be_h = nc.dram_tensor("ln_b", [1, D], F32, kind="ExternalInput").ap()
    wq_h = nc.dram_tensor("wq", [D, D], F32, kind="ExternalInput").ap()
    wk_h = nc.dram_tensor("wk", [D, D], F32, kind="ExternalInput").ap()
    wv_h = nc.dram_tensor("wv", [D, D], F32, kind="ExternalInput").ap()
    wo_h = nc.dram_tensor("wo", [D, D], F32, kind="ExternalInput").ap()
    pk_h = nc.dram_tensor("pk", [D // 4, D], I32, kind="ExternalInput").ap()
    ws_h = nc.dram_tensor("wsc", [1, 1], F32, kind="ExternalInput").ap()
    fb_h = nc.dram_tensor("fbias", [1, D], F32, kind="ExternalInput").ap()
    c_h = nc.dram_tensor("cvec", [1, D], F32, kind="ExternalInput").ap()
    wm_h = nc.dram_tensor("wmod", [D, 6 * D], F32, kind="ExternalInput").ap()
    out_h = nc.dram_tensor("out", [S, D], F32, kind="ExternalOutput").ap()
    scr = nc.dram_tensor("scr_rows", [12, S], F32, kind="Internal").ap()

    es = ExitStack()
    SB = lambda n, sh, dt: es.enter_context(nc.sbuf_tensor(n, sh, dt))

    x_tok = SB("x_tok", [128, TC, D], F32)
    PS = es.enter_context(nc.psum_tensor("PS", [128, 8, 512], F32))
    sc1 = SB("sc1", [128, 1], F32)
    schalf = SB("schalf", [128, 2], F32)
    identf = SB("identf", [1, 1], F32)
    identb = SB("identb", [1, 1], BF16)
    ident128 = SB("ident128", [128, 128], BF16)
    ij128 = SB("ij128", [128, 128], I32)
    onesr = SB("onesr", [1, 128], BF16)
    scT = SB("scT", [128, 8], BF16)
    hsel = SB("hsel", [128, 2], BF16)
    bias_m = SB("bias_m", [128, 2], F32)
    nq2t = SB("nq2t", [2, 2], F32)
    negm = SB("negm", [2, 1], F32)
    negmT = SB("negmT", [1, 2], BF16)
    identf2 = SB("identf2", [2, 2], F32)
    wsb = SB("wsb", [128, 1], F32)
    scal = SB("scal", [1, 16], F32)
    eps1 = SB("eps1", [1, 1], F32)
    mu3 = SB("mu3", [128, TC], F32)
    r3c = SB("r3c", [128, TC], F32)
    asb = SB("asb", [128, TC], F32)
    rasb = SB("rasb", [128, TC], F32)
    stx = SB("stx", [128, TC], F32)
    stxx = SB("stxx", [128, TC], F32)
    mcol = SB("mcol", [128, 2], F32)
    wt_ffn = SB("wt_ffn", [128, FC, D], BF16)

    sems = {}
    for s_ in ("ld", "lg", "dt", "tv", "vv", "sv", "st", "wr", "ig",
               "bc", "pk", "d1"):
        sems[s_] = Sem(es, nc, "sm_" + s_)
    ld, lg, dts = sems["ld"], sems["lg"], sems["dt"]
    tv, vv, sv = sems["tv"], sems["vv"], sems["sv"]
    st, wr, ig = sems["st"], sems["wr"], sems["ig"]
    bc, pk, dt1 = sems["bc"], sems["pk"], sems["d1"]

    blk = BlkWrap(es.enter_context(nc.Block()), {"v": set(), "s": set()})

    def r3d(ap, w=512):
        return ap.rearrange("p (a b) -> p a b", b=w)

    with es:
        @blk.sync
        def _(sy):
            for tc in range(TC):
                sy.dma_start(x_tok[:, tc, :], x_h[tc * 128:(tc + 1) * 128, :]
                             ).then_inc(ld.h, 16)
                ld.n += 16
        x_loaded = ld.n

        esAI = ExitStack()
        ai = esAI.enter_context(nc.sbuf_tensor("ai", [128, FC, S], BF16, side="right"))
        es1 = ExitStack()
        S1 = lambda n, sh, dt: es1.enter_context(nc.sbuf_tensor(n, sh, dt))
        pool = S1("pool", [1, 10, 512], F32)
        uab = S1("uab", [1, S], BF16)
        agb = S1("agb", [1, D], BF16)
        Blhs = S1("Blhs", [4, D], BF16)
        Brhs = S1("Brhs", [4, S], BF16)
        statx = S1("statx", [128, FC, 4], BF16)
        statxx = S1("statxx", [128, FC, 2], BF16)
        es1a = ExitStack()
        S1A = lambda n, sh, dt: es1a.enter_context(nc.sbuf_tensor(n, sh, dt))
        vrow = S1A("vrow", [1, 12, D], BF16)
        frow = S1A("frow", [1, 4, D], F32)
        rowf = S1A("rowf", [1, D], F32)
        wmbuf = S1A("wmbuf", [128, 2, 3072], BF16)

        @blk.gpsimd
        def _(g):
            g.dma_start(wsb[:, 0:1], ws_h.partition_broadcast(128)).then_inc(lg.h, 16)
            g.dma_start(frow[0:1, 0, :], c_h).then_inc(lg.h, 16)
            g.dma_start(frow[0:1, 2, :], g_h).then_inc(lg.h, 16)
            g.dma_start(frow[0:1, 3, :], be_h).then_inc(lg.h, 16)
            g.dma_start(vrow[0:1, 11, :], fb_h).then_inc(lg.h, 16)
            g.iota(ij128[:], [[1, 128]], base=0, channel_multiplier=-1).then_inc(ig.h, 1)
        lg.n += 80
        ig.n += 1
        base_ld = lg.n

        @blk.vector
        def _(v):
            v.memset(sc1[:], EPS)
            v.memset(eps1[:], EPS)
            v.memset(schalf[:, 0:1], -0.5)
            v.memset(schalf[:, 1:2], 0.5)
            v.memset(identf[:], 1.0)
            v.memset(identb[:], 1.0)
            v.memset(onesr[:], 1.0)
            v.memset(hsel[:, 0:1], 1.0)
            v.memset(hsel[:, 1:2], 0.0)
            v.memset(statx[:, :, 0:1], 1.0)
            v.memset(statxx[:, :, 0:1], 1.0)
            v.wait_ge(lg.h, base_ld)
            v.wait_ge(ig.h, 1)
            v.tensor_scalar(ident128[:], ij128[:], 0, None, ALU.is_equal)
            v.tensor_scalar(identf2[:], ij128[0:2, 0:2], 0, None, ALU.is_equal
                            ).then_inc(vv.h, 1)
            vv.n += 1
        consts_done = vv.n

        @blk.scalar
        def _(s):
            s.wait_ge(lg.h, base_ld)
            s.activation(frow[0:1, 1, :], frow[0:1, 0, :], AF.Silu
                         ).then_inc(sv.h, 1)
            sv.n += 1
        silu_done = sv.n

        @blk.tensor
        def _(t):
            t.wait_ge(sv.h, silu_done)
            t.wait_ge(vv.h, consts_done)
            for k in range(8):
                ins = t.transpose(PS[:, 6, k:k + 1], frow[0:1, 1, k * 128:(k + 1) * 128],
                                  identf[0:1, 0:1])
            ins.then_inc(tv.h, 1)
            tv.n += 1
        scT_tp = tv.n

        @blk.vector
        def _(v):
            v.wait_ge(tv.h, scT_tp)
            v.tensor_copy(scT[:, 0:8], PS[:, 6, 0:8]).then_inc(vv.h, 1)
            vv.n += 1
        scT_done = vv.n

        mod_mm_hist = []
        mod_evac = [scT_done]
        chunk_ld = []
        for hf in range(2):
            for kc in range(FC):
                i = hf * FC + kc

                psem = lg if i % 2 == 0 else bc

                @blk.gpsimd
                def _(g, hf=hf, kc=kc, i=i, psem=psem):
                    if i == 0:
                        g.wait_ge(lg.h, base_ld)
                    if i >= 2:
                        g.wait_ge(tv.h, mod_mm_hist[i - 2])
                    g.dma_start(wmbuf[:, i % 2, :],
                                wm_h[kc * 128:(kc + 1) * 128,
                                     hf * 3072:(hf + 1) * 3072]
                                ).then_inc(psem.h, 16)
                psem.n += 16
                chunk_ld.append((psem, psem.n))

                @blk.tensor
                def _(t, hf=hf, kc=kc, i=i):
                    sem_i, cnt_i = chunk_ld[i]
                    t.wait_ge(sem_i.h, cnt_i)
                    if kc == 0:
                        t.wait_ge(vv.h, mod_evac[-1])
                    for ncol in range(6):
                        ins = t.matmul(PS[0:1, ncol, :], scT[:, kc:kc + 1],
                                       wmbuf[:, kc % 2, ncol * 512:(ncol + 1) * 512],
                                       start=(kc == 0), stop=(kc == FC - 1))
                    ins.then_inc(tv.h, 1)
                    tv.n += 1
                mod_mm_hist.append(tv.n)

            @blk.vector
            def _(v, hf=hf):
                v.wait_ge(tv.h, mod_mm_hist[-1])
                if hf == 0:
                    for j in range(3):
                        ins = v.tensor_copy(r3d(vrow[0:1, j, :]), PS[0:1, 2 * j:2 * j + 2, :])
                else:
                    v.tensor_copy(r3d(rowf[0:1, :]), PS[0:1, 0:2, :])
                    ins = v.tensor_copy(r3d(vrow[0:1, 5, :]), PS[0:1, 4:6, :])
                ins.then_inc(vv.h, 1)
                vv.n += 1
            mod_evac.append(vv.n)
        mod_h1_evac = mod_evac[-1]

        @blk.sync
        def _(sy):
            sy.wait_ge(vv.h, mod_h1_evac)
            sy.dma_start(scr[5:6, 0:D], rowf[0:1, :]).then_inc(wr.h, 16)
            wr.n += 16
        sb_written = wr.n

        @blk.vector
        def _(v):
            v.wait_ge(wr.h, sb_written)
            v.tensor_scalar(r3d(rowf[0:1, :]), PS[0:1, 2:4, :], 1.0, None, ALU.add
                            ).then_inc(vv.h, 1)
            vv.n += 1
        cb1_row = vv.n

        @blk.sync
        def _(sy):
            sy.wait_ge(vv.h, cb1_row)
            sy.dma_start(scr[4:5, 0:D], rowf[0:1, :]).then_inc(wr.h, 16)
            wr.n += 16
        cb1_written = wr.n

        @blk.vector
        def _(v):
            v.wait_ge(wr.h, cb1_written)
            v.tensor_scalar(vrow[0:1, 6, :], vrow[0:1, 1, :], 1.0, None, ALU.add)
            for (src, dst) in ((2, 7), (5, 8)):
                v.tensor_scalar(vrow[0:1, 9, :], vrow[0:1, src, :], 0.0, None, ALU.is_ge)
                v.tensor_scalar(vrow[0:1, 9, :], vrow[0:1, 9, :], 2.0, -1.0, ALU.mult, ALU.add)
                v.tensor_scalar(vrow[0:1, 10, :], vrow[0:1, src, :], -1.0, None, ALU.mult)
                v.tensor_tensor(vrow[0:1, 10, :], vrow[0:1, src, :], vrow[0:1, 10, :], ALU.max)
                v.tensor_scalar_max(vrow[0:1, 10, :], vrow[0:1, 10, :], 0.25)
                v.tensor_tensor(vrow[0:1, dst, :], vrow[0:1, 9, :], vrow[0:1, 10, :], ALU.mult)
            v.reduce_sum(scal[0:1, 0:1], vrow[0:1, 6, :], AX.X)
            v.tensor_tensor(vrow[0:1, 9, :], vrow[0:1, 6, :], vrow[0:1, 6, :], ALU.mult)
            v.reduce_sum(scal[0:1, 1:2], vrow[0:1, 9, :], AX.X)
            v.tensor_tensor(vrow[0:1, 10, :], vrow[0:1, 6, :], vrow[0:1, 0, :], ALU.mult)
            v.reduce_sum(scal[0:1, 2:3], vrow[0:1, 10, :], AX.X)
            v.tensor_tensor(vrow[0:1, 4, :], vrow[0:1, 0, :], vrow[0:1, 0, :], ALU.mult)
            v.reduce_sum(scal[0:1, 3:4], vrow[0:1, 4, :], AX.X)
            v.reduce_sum(scal[0:1, 4:5], vrow[0:1, 0, :], AX.X)
            v.drain()
            v.tensor_scalar(scal[0:1, 3:4], scal[0:1, 3:4], 1.0 / D, None, ALU.mult)
            v.tensor_scalar(scal[0:1, 4:5], scal[0:1, 4:5], 1.0 / D, None, ALU.mult
                            ).then_inc(vv.h, 1)
            vv.n += 1
        drows_done = vv.n

        @blk.tensor
        def _(t):
            t.wait_ge(vv.h, drows_done)
            psb = PS[:, 7, :].bitcast(BF16)
            for (j, r) in enumerate((6, 9, 10)):
                for k in range(8):
                    c = 2 * (j * 8 + k)
                    ins = t.transpose(psb[:, c:c + 1],
                                      vrow[0:1, r, k * 128:(k + 1) * 128], identb[0:1, 0:1])
            ins.then_inc(tv.h, 1)
            tv.n += 1
        stat_tp = tv.n

        @blk.vector
        def _(v):
            v.wait_ge(tv.h, stat_tp)
            psb = PS[:, 7, :].bitcast(BF16)
            v.tensor_copy(statx[:, :, 1:2], psb[:, 0:16:2].unsqueeze(2))
            v.tensor_copy(statx[:, :, 2:3], psb[:, 16:32:2].unsqueeze(2))
            v.tensor_copy(statx[:, :, 3:4], psb[:, 32:48:2].unsqueeze(2))
            v.tensor_copy(statxx[:, :, 1:2], psb[:, 16:32:2].unsqueeze(2)).then_inc(vv.h, 1)
            vv.n += 1
        stat_lhs_done = vv.n

        def row_to_scr(make_row, scr_row, width=D):
            @blk.vector
            def _(v):
                make_row(v)
                v.engine_nop().then_inc(vv.h, 1)
                vv.n += 1
            r_t = vv.n

            @blk.sync
            def _(sy, r_t=r_t, scr_row=scr_row, width=width):
                sy.wait_ge(vv.h, r_t)
                sy.dma_start(scr[scr_row:scr_row + 1, 0:width], rowf[0:1, 0:width]
                             ).then_inc(wr.h, 16)
                wr.n += 16
            return wr.n

        def _mk(slot):
            def f(v):
                v.wait_ge(wr.h, last_w[0])
                v.tensor_copy(rowf[0:1, :], vrow[0:1, slot, :])
            return f
        last_w = [cb1_written]
        last_w[0] = row_to_scr(_mk(7), 6)
        last_w[0] = row_to_scr(_mk(8), 7)

        def _mk_gbias(v):
            v.wait_ge(wr.h, last_w[0])
            v.tensor_tensor(rowf[0:1, :], vrow[0:1, 8, :], vrow[0:1, 11, :], ALU.mult)
        last_w[0] = row_to_scr(_mk_gbias, 8)

        def _mk_sg(v):
            v.wait_ge(wr.h, last_w[0])
            v.tensor_tensor(rowf[0:1, :], vrow[0:1, 0, :], frow[0:1, 2, :], ALU.mult)
        last_w[0] = row_to_scr(_mk_sg, 1)

        def _mk_ag(v):
            v.wait_ge(wr.h, last_w[0])
            v.tensor_tensor(rowf[0:1, :], vrow[0:1, 6, :], frow[0:1, 2, :], ALU.mult)
            v.tensor_copy(agb[0:1, :], rowf[0:1, :])
        last_w[0] = row_to_scr(_mk_ag, 2)

        @blk.sync
        def _(sy):
            sy.dma_start(scr[3:4, 0:D], frow[0:1, 2, :]).then_inc(wr.h, 16)
            sy.dma_start(scr[0:1, 0:D], frow[0:1, 3, :]).then_inc(wr.h, 16)
            wr.n += 32
        blhs_written = wr.n

        @blk.gpsimd
        def _(g):
            g.wait_ge(wr.h, blhs_written)
            g.dma_start(Blhs[0:4, 0:D], scr[0:4, 0:D]).then_inc(lg.h, 16)
        lg.n += 16
        blhs_ld = lg.n

        es1a.close()
        es1b = ExitStack()
        S1B = lambda n, sh, dt: es1b.enter_context(nc.sbuf_tensor(n, sh, dt))
        xfe = S1B("xfe", [128, FC, S], BF16)
        xbf = S1B("xbf", [128, 2, D], BF16)
        x2c = S1B("x2c", [128, 2, 512], BF16)
        ais = S1B("ais", [128, S], F32)
        dmat_hist = []
        for tc in range(TC):
            @blk.vector
            def _(v, tc=tc):
                if tc == 0:
                    v.wait_ge(ld.h, x_loaded)
                    v.wait_ge(tv.h, stat_tp)
                    v.wait_ge(wr.h, blhs_written)
                    v.wait_ge(lg.h, blhs_ld)
                elif tc >= 2:
                    v.wait_ge(dts.h, dmat_hist[tc - 2])
                v.tensor_copy(xbf[:, tc % 2, :], x_tok[:, tc, :]).then_inc(vv.h, 1)
                vv.n += 1
            xbf_c = vv.n

            @blk.sync
            def _(sy, tc=tc, xbf_c=xbf_c):
                sy.wait_ge(vv.h, xbf_c)
                sy.dma_start_transpose(xfe[:, :, tc * 128:(tc + 1) * 128],
                                       xbf[:, tc % 2, :]).then_inc(dts.h, 16)
                dts.n += 16
            dmat_hist.append(dts.n)
        xfe_done = dts.n

        stat_ev_hist = []
        srow_hist = []
        P = lambda i: pool[0:1, i, :]
        SA, SAA, SAS = scal[0:1, 0:1], scal[0:1, 1:2], scal[0:1, 2:3]
        SSS_D, SS_D = scal[0:1, 3:4], scal[0:1, 4:5]
        for nq in range(NQ):
            q0 = nq * 512
            sq_hist = []
            mm_hist = []
            for fc in range(FC):
                @blk.vector
                def _(v, fc=fc, nq=nq, q0=q0):
                    if fc == 0:
                        v.wait_ge(dts.h, dmat_hist[nq * 4 + 3])
                    if nq == 0 and fc == 0:
                        v.wait_ge(vv.h, stat_lhs_done)
                    if fc >= 2:
                        v.wait_ge(tv.h, mm_hist[fc - 2])
                    v.tensor_tensor(x2c[:, fc % 2, :], xfe[:, fc, q0:q0 + 512],
                                    xfe[:, fc, q0:q0 + 512], ALU.mult).then_inc(vv.h, 1)
                    vv.n += 1
                sq_hist.append(vv.n)

                @blk.tensor
                def _(t, fc=fc, nq=nq, q0=q0):
                    t.wait_ge(vv.h, sq_hist[fc])
                    if fc == 0 and nq > 0:
                        t.wait_ge(vv.h, stat_ev_hist[nq - 1])
                    for s_i in range(4):
                        t.matmul(PS[0:1, s_i, :], statx[:, fc, s_i:s_i + 1],
                                 xfe[:, fc, q0:q0 + 512],
                                 start=(fc == 0), stop=(fc == FC - 1))
                    for s_i in range(2):
                        ins = t.matmul(PS[0:1, 4 + s_i, :], statxx[:, fc, s_i:s_i + 1],
                                       x2c[:, fc % 2, :],
                                       start=(fc == 0), stop=(fc == FC - 1))
                    ins.then_inc(tv.h, 1)
                    tv.n += 1
                mm_hist.append(tv.n)
            stats_done = tv.n

            @blk.vector
            def _(v, stats_done=stats_done, nq=nq):
                v.wait_ge(tv.h, stats_done)
                if nq >= 1:
                    v.wait_ge(wr.h, srow_hist[nq - 1])
                for s_i in range(6):
                    ins = v.tensor_copy(P(s_i), PS[0:1, s_i, :])
                ins.then_inc(vv.h, 1)
                vv.n += 1
            stat_ev_hist.append(vv.n)

            @blk.vector
            def _(v):
                v.tensor_scalar(P(0), P(0), 1.0 / D, None, ALU.mult)
                v.tensor_tensor(P(6), P(0), P(0), ALU.mult)
                v.scalar_tensor_tensor(P(7), P(4), 1.0 / D, P(6), ALU.mult, ALU.subtract)
                v.engine_nop().then_inc(vv.h, 1)
                vv.n += 1
            alg1 = vv.n

            @blk.scalar
            def _(s, alg1=alg1):
                s.wait_ge(vv.h, alg1)
                s.activation(P(8), P(7), AF.Ln, bias=eps1[0:1, 0:1])
                s.activation(P(7), P(8), AF.Exp, scale=-0.5).then_inc(sv.h, 1)
                sv.n += 1
            alg_s1 = sv.n

            @blk.vector
            def _(v, alg_s1=alg_s1):
                v.wait_ge(sv.h, alg_s1)
                v.scalar_tensor_tensor(P(6), P(0), SA, P(1), ALU.mult, ALU.subtract)
                v.tensor_tensor(P(8), P(6), P(7), ALU.mult)
                v.tensor_scalar(P(8), P(8), -1.0 / D, SS_D, ALU.mult, ALU.add)
                v.scalar_tensor_tensor(P(6), P(0), SAA, P(2), ALU.mult, ALU.subtract)
                v.tensor_tensor(P(9), P(0), P(6), ALU.mult)
                v.tensor_tensor(P(6), P(0), P(2), ALU.mult)
                v.tensor_tensor(P(2), P(5), P(9), ALU.add)
                v.tensor_tensor(P(9), P(2), P(6), ALU.subtract)
                v.scalar_tensor_tensor(P(6), P(0), SAS, P(3), ALU.mult, ALU.subtract)
                v.tensor_tensor(P(2), P(7), P(6), ALU.mult)
                v.tensor_tensor(P(6), P(7), P(7), ALU.mult)
                v.tensor_tensor(P(3), P(6), P(9), ALU.mult)
                v.scalar_tensor_tensor(P(9), P(2), -2.0, P(3), ALU.mult, ALU.add)
                v.tensor_scalar(P(9), P(9), 1.0 / D, SSS_D, ALU.mult, ALU.add)
                v.tensor_tensor(P(6), P(8), P(8), ALU.mult)
                v.tensor_tensor(P(9), P(9), P(6), ALU.subtract)
                v.engine_nop().then_inc(vv.h, 1)
                vv.n += 1
            alg2 = vv.n

            @blk.scalar
            def _(s, alg2=alg2):
                s.wait_ge(vv.h, alg2)
                s.activation(P(6), P(9), AF.Ln, bias=eps1[0:1, 0:1])
                s.activation(P(9), P(6), AF.Exp, scale=-0.5).then_inc(sv.h, 1)
                sv.n += 1
            alg_s2 = sv.n

            @blk.vector
            def _(v, q0=q0, alg_s2=alg_s2):
                v.wait_ge(sv.h, alg_s2)
                v.tensor_tensor(P(6), P(7), P(9), ALU.mult)
                v.tensor_copy(uab[0:1, q0:q0 + 512], P(6))
                v.tensor_tensor(P(2), P(0), P(6), ALU.mult)
                v.tensor_scalar(P(2), P(2), -1.0, None, ALU.mult)
                v.tensor_tensor(P(3), P(8), P(9), ALU.mult)
                v.tensor_scalar(P(3), P(3), -1.0, None, ALU.mult)
                v.engine_nop().then_inc(vv.h, 1)
                vv.n += 1
            alg3 = vv.n

            @blk.sync
            def _(sy, q0=q0, alg3=alg3, nq=nq):
                sy.wait_ge(vv.h, alg3)
                if nq == 0:
                    sy.wait_ge(wr.h, blhs_written)
                sy.dma_start(scr[9:10, q0:q0 + 512], P(9)).then_inc(wr.h, 16)
                sy.dma_start(scr[10:11, q0:q0 + 512], P(2)).then_inc(wr.h, 16)
                sy.dma_start(scr[11:12, q0:q0 + 512], P(3)).then_inc(wr.h, 16)
                wr.n += 48
            srow_hist.append(wr.n)
        brhs_written = wr.n

        @blk.gpsimd
        def _(g):
            g.wait_ge(wr.h, brhs_written)
            g.dma_start(Brhs[1:4, 0:S], scr[9:12, 0:S]).then_inc(lg.h, 16)
        lg.n += 16
        brhs_ld = lg.n

        @blk.vector
        def _(v):
            v.wait_ge(lg.h, brhs_ld)
            v.memset(Brhs[0:1, :], 1.0).then_inc(vv.h, 1)
            vv.n += 1
        brhs_done = vv.n

        ab_hist = []
        ap_hist = []
        for fc in range(FC):
            @blk.tensor
            def _(t, fc=fc):
                if fc == 0:
                    t.wait_ge(vv.h, brhs_done)
                    t.wait_ge(lg.h, blhs_ld)
                else:
                    t.wait_ge(vv.h, ap_hist[fc - 1])
                for nq in range(NQ):
                    t.matmul(PS[:, nq, :], agb[0:1, fc * 128:(fc + 1) * 128],
                             uab[0:1, nq * 512:(nq + 1) * 512], start=True, stop=True)
                    ins = t.matmul(PS[:, 4 + nq, :], Blhs[0:4, fc * 128:(fc + 1) * 128],
                                   Brhs[0:4, nq * 512:(nq + 1) * 512], start=True, stop=True)
                ins.then_inc(tv.h, 1)
                tv.n += 1
            ab_hist.append(tv.n)

            @blk.vector
            def _(v, fc=fc):
                v.wait_ge(tv.h, ab_hist[fc])
                v.tensor_tensor(r3d(ais[:, :]), r3d(xfe[:, fc, :]), PS[:, 0:4, :], ALU.mult)
                v.tensor_tensor(r3d(ai[:, fc, :]), r3d(ais[:, :]), PS[:, 4:8, :], ALU.add
                                ).then_inc(vv.h, 1)
                vv.n += 1
            ap_hist.append(vv.n)
        ai_done = vv.n

        es1b.close()
        es1.close()

        esL2 = ExitStack()
        attn_n = esL2.enter_context(nc.sbuf_tensor("attn_n", [128, NP, S], BF16))
        es2 = ExitStack()
        S2 = lambda n, sh, dt: es2.enter_context(nc.sbuf_tensor(n, sh, dt))
        expab2 = S2("expab2", [128, 2, 2048], BF16)
        pvstage = S2("pvstage", [128, 2, 512], F32)
        recT = S2("recT", [128, 512], BF16)
        ones2 = S2("ones2", [128, 128], BF16)
        q_p = S2("q_p", [128, S], BF16)
        k_p = S2("k_p", [128, S], BF16)
        v_p = S2("v_p", [128, TC, 193], BF16)
        v_fe = S2("v_fe", [128, S], BF16)
        v_t = S2("v_t", [128, TC, 128], BF16)
        wst = S2("wst", [128, 3, FC, 128], BF16)
        pkb = S2("pkb", [128, 2, 1024], I32)
        encb = S2("encb", [128, D], I32)
        ternf = S2("ternf", [128, D], BF16)

        @blk.vector
        def _(v):
            v.drain()
            v.memset(v_p[:, :, 64:66], 1.0)
            v.memset(v_p[:, :, 66:129], 0.0)
            v.memset(ones2[0:1, :], 1.0)
            v.memset(ones2[64:65, :], 1.0).then_inc(vv.h, 1)
            vv.n += 1
        vp_init = vv.n

        pk_lds = []
        for rb in range(2):
            @blk.gpsimd
            def _(g, rb=rb):
                if rb == 0:
                    g.wait_ge(vv.h, ai_done)
                else:
                    g.wait_ge(pk.h, pk_lds[0])
                g.dma_start(pkb[:, rb, :], pk_h[rb * 128:(rb + 1) * 128, :]
                            ).then_inc(pk.h, 16)
            pk.n += 16
            pk_lds.append(pk.n)

        up_ev_hist = [ai_done]
        up_tp_hist = []
        for i in range(8):
            rb, s_i = i // 4, i % 4
            oc = 2 * s_i + rb

            @blk.vector
            def _(v, rb=rb, s_i=s_i, i=i):
                v.wait_ge(pk.h, pk_lds[rb])
                if i > 0:
                    v.wait_ge(tv.h, up_tp_hist[-1])
                v.tensor_scalar(encb[:, :], pkb[:, rb, :], 2 * s_i, 3,
                                ALU.logical_shift_right, ALU.bitwise_and)
                v.tensor_scalar(ternf[:, :], encb[:, :], 1.0, None, ALU.subtract
                                ).then_inc(vv.h, 1)
                vv.n += 1
            up_sq = vv.n

            @blk.tensor
            def _(t, i=i, up_sq=up_sq):
                t.wait_ge(vv.h, up_sq)
                if i == 0:
                    t.wait_ge(vv.h, ai_done)
                else:
                    t.wait_ge(vv.h, up_ev_hist[-1])
                psb = PS[:, 6:8, :].rearrange("p a b -> p (a b)").bitcast(BF16)
                for ic in range(FC):
                    ins = t.transpose(psb[:, ic * 128:(ic + 1) * 128],
                                      ternf[:, ic * 128:(ic + 1) * 128], ident128[:, :])
                ins.then_inc(tv.h, 1)
                tv.n += 1
            up_tp_hist.append(tv.n)
            up_tp = tv.n

            @blk.vector
            def _(v, oc=oc, up_tp=up_tp):
                v.wait_ge(tv.h, up_tp)
                psb = PS[:, 6:8, :].rearrange("p a b -> p (a b)").bitcast(BF16)
                v.tensor_copy(wt_ffn[:, :, oc * 128:(oc + 1) * 128],
                              psb[:, 0:1024].rearrange("p (a b) -> p a b", b=128)
                              ).then_inc(vv.h, 1)
                vv.n += 1
            up_ev_hist.append(vv.n)
        unpack_done = vv.n

        prev_proj_done = [0]
        prev_vp_ready = [0]
        prev_exp_tail = [0]
        prev_pv15 = [0]
        prev_norm_done = [0]
        last_rec_done = [0]
        last_bias_done = [0]
        deferred_bc = [None]
        for p in range(NP):
            @blk.gpsimd
            def _(g, p=p):
                if p == 0:
                    g.wait_ge(vv.h, ai_done)
                else:
                    g.wait_ge(tv.h, prev_proj_done[0])
                for (wi, wh) in ((0, wq_h), (1, wk_h), (2, wv_h)):
                    src = wh[:, p * 128:(p + 1) * 128]
                    for fc in range(FC):
                        g.dma_start(wst[:, wi, fc, :],
                                    src[fc * 128:(fc + 1) * 128, :]).then_inc(lg.h, 16)
            lg.n += 16 * 24
            wst_ld = lg.n

            qk_ev = []
            for (wi, dest) in ((0, q_p), (1, k_p)):
                @blk.tensor
                def _(t, p=p, wi=wi, wst_ld=wst_ld):
                    t.wait_ge(lg.h, wst_ld)
                    if p == 0 and wi == 0:
                        t.wait_ge(vv.h, unpack_done)
                        t.wait_ge(vv.h, ai_done)
                    if wi == 0 and p > 0:
                        t.wait_ge(sv.h, prev_exp_tail[0])
                    if wi == 1:
                        t.wait_ge(vv.h, qk_ev[0])
                    for nq in range(NQ):
                        for fc in range(FC):
                            ins = t.matmul(PS[:, nq, :], wst[:, wi, fc, :],
                                           ai[:, fc, nq * 512:(nq + 1) * 512],
                                           start=(fc == 0), stop=(fc == FC - 1))
                    ins.then_inc(tv.h, 1)
                    tv.n += 1
                mm_t = tv.n

                @blk.vector
                def _(v, dest=dest, mm_t=mm_t):
                    v.wait_ge(tv.h, mm_t)
                    v.tensor_scalar(r3d(dest[:, :]), PS[:, 0:4, :], QKS, None, ALU.mult
                                    ).then_inc(vv.h, 1)
                    vv.n += 1
                qk_ev.append(vv.n)
            k_ev = qk_ev[1]

            @blk.tensor
            def _(t, p=p):
                t.wait_ge(vv.h, qk_ev[1])
                for nq in range(NQ):
                    for fc in range(FC):
                        ins = t.matmul(PS[:, nq, :], wst[:, 2, fc, :],
                                       ai[:, fc, nq * 512:(nq + 1) * 512],
                                       start=(fc == 0), stop=(fc == FC - 1))
                ins.then_inc(tv.h, 1)
                tv.n += 1
            v_mm = tv.n

            @blk.vector
            def _(v, v_mm=v_mm):
                v.wait_ge(tv.h, v_mm)
                v.tensor_copy(r3d(v_fe[:, :]), PS[:, 0:4, :]).then_inc(vv.h, 1)
                vv.n += 1
            v_fe_done = vv.n

            @blk.sync
            def _(sy, v_fe_done=v_fe_done, p=p):
                sy.wait_ge(vv.h, v_fe_done)
                if p > 0:
                    sy.wait_ge(vv.h, prev_vp_ready[0])
                sy.dma_start_transpose(v_t[:, :, :], v_fe[:, :]).then_inc(dts.h, 16)
                dts.n += 16
            v_tp = dts.n

            proj_done = tv.n
            prev_proj_done[0] = proj_done

            nrm_mm = []
            sq_hist = []
            for (j, srcb) in ((0, q_p), (1, k_p)):
                for nq in range(NQ):
                    i2 = j * NQ + nq

                    @blk.vector
                    def _(v, j=j, srcb=srcb, nq=nq, i2=i2, k_ev=k_ev, p=p):
                        if i2 == 0:
                            v.wait_ge(vv.h, k_ev)
                            if p > 0:
                                v.wait_ge(tv.h, prev_pv15[0])
                        if i2 >= 3:
                            v.wait_ge(tv.h, nrm_mm[i2 - 3])
                        sl = expab2[:, (i2 % 4) // 2,
                                    512 * (i2 % 2):512 * (i2 % 2) + 512]
                        v.tensor_tensor(sl,
                                        srcb[:, nq * 512:(nq + 1) * 512],
                                        srcb[:, nq * 512:(nq + 1) * 512], ALU.mult
                                        ).then_inc(vv.h, 1)
                        vv.n += 1
                    sq_hist.append(vv.n)

                    @blk.tensor
                    def _(t, j=j, nq=nq, i2=i2):
                        t.wait_ge(vv.h, sq_hist[i2])
                        sl = expab2[:, (i2 % 4) // 2,
                                    512 * (i2 % 2):512 * (i2 % 2) + 512]
                        t.matmul(PS[0:1, nq, :], hsel[:, 0:1], sl,
                                 start=True, stop=True).then_inc(tv.h, 1)
                        tv.n += 1
                    nrm_mm.append(tv.n)

                @blk.vector
                def _(v, j=j):
                    v.wait_ge(tv.h, nrm_mm[j * NQ + NQ - 1])
                    v.reduce_max(nq2t[0:1, j:j + 1],
                                 PS[0:1, 0:4, :].rearrange("p a b -> p (a b)"), AX.X
                                 ).then_inc(vv.h, 1)
                    vv.n += 1
            @blk.vector
            def _(v):
                v.drain()
                v.tensor_tensor(negm[0:1, 0:1], nq2t[0:1, 0:1], nq2t[0:1, 1:2], ALU.mult
                                ).then_inc(vv.h, 1)
                vv.n += 1
            m2_d = vv.n

            @blk.scalar
            def _(s, m2_d=m2_d):
                s.wait_ge(vv.h, m2_d)
                s.activation(negm[0:1, 0:1], negm[0:1, 0:1], AF.Ln, bias=sc1[0:1, 0:1])
                s.activation(negm[0:1, 0:1], negm[0:1, 0:1], AF.Exp, scale=0.5
                             ).then_inc(sv.h, 1)
                sv.n += 1
            m_d = sv.n

            @blk.vector
            def _(v, m_d=m_d):
                v.wait_ge(sv.h, m_d)
                v.tensor_scalar(negmT[0:1, 0:1], negm[0:1, 0:1], -1.0, None, ALU.mult
                                ).then_inc(vv.h, 1)
                vv.n += 1
            negm_d = vv.n

            @blk.tensor
            def _(t, negm_d=negm_d, p=p):
                t.wait_ge(vv.h, negm_d)
                if p > 0:
                    t.wait_ge(vv.h, last_rec_done[0])
                t.matmul(PS[:, 7, 0:1], ones2[0:1, 0:128], negmT[0:1, 0:1],
                         start=True, stop=True).then_inc(tv.h, 1)
                tv.n += 1
            bb = tv.n

            @blk.vector
            def _(v, bb=bb):
                v.wait_ge(tv.h, bb)
                v.tensor_copy(bias_m[:, 0:1], PS[:, 7, 0:1]).then_inc(vv.h, 1)
                vv.n += 1
            bias_done = vv.n
            last_bias_done[0] = bias_done

            @blk.vector
            def _(v, v_tp=v_tp, p=p):
                v.wait_ge(dts.h, v_tp)
                if p == 0:
                    v.wait_ge(vv.h, vp_init)
                else:
                    v.wait_ge(tv.h, prev_pv15[0])
                v.tensor_copy(v_p[:, :, 0:64], v_t[:, :, 0:64])
                v.tensor_copy(v_p[:, :, 129:193], v_t[:, :, 64:128]).then_inc(vv.h, 1)
                vv.n += 1
            vp_ready = vv.n
            prev_vp_ready[0] = vp_ready

            for nq4 in range(NQ):
                q0 = nq4 * 512
                pva = 4 + 2 * (nq4 % 2)
                qkd = {}
                pvd = {}
                pv15_prev = prev_pv15[0]
                sv_base = sv.n
                expd = {kc: sv_base + kc + 1 for kc in range(KC)}

                def emit_qk(t, kc, q0=q0, qkd=qkd):
                    bA = 2 * (kc % 2)
                    t.matmul(PS[:, bA, :], k_p[0:64, kc * 128:(kc + 1) * 128],
                             q_p[0:64, q0:q0 + 512], start=True, stop=True)
                    t.matmul(PS[:, bA + 1, :], k_p[64:128, kc * 128:(kc + 1) * 128],
                             q_p[64:128, q0:q0 + 512], start=True, stop=True,
                             tile_position=(64, 0)).then_inc(tv.h, 1)
                    tv.n += 1
                    qkd[kc] = tv.n

                @blk.tensor
                def _(t, p=p, nq4=nq4, bias_done=bias_done):
                    if nq4 == 0:
                        t.wait_ge(vv.h, bias_done)
                    else:
                        t.wait_ge(sv.h, prev_exp_tail[0])
                    emit_qk(t, 0)
                    emit_qk(t, 1)

                for kc in range(KC):
                    @blk.tensor
                    def _(t, kc=kc, p=p, nq4=nq4, pva=pva, vp_ready=vp_ready):
                        if kc == 0 and prev_norm_done[0]:
                            t.wait_ge(vv.h, prev_norm_done[0])
                        if kc == 0 and nq4 == 0:
                            t.wait_ge(vv.h, vp_ready)
                        t.wait_ge(sv.h, expd[kc])
                        sl = expab2[:, (kc % 4) // 2, 1024 * (kc % 2):1024 * (kc % 2) + 1024]
                        t.matmul(PS[:, pva, :], v_p[:, kc, 0:128], sl[:, 0:512],
                                 start=(kc == 0), stop=(kc == KC - 1))
                        t.matmul(PS[:, pva + 1, :], v_p[:, kc, 65:193],
                                 sl[:, 512:1024],
                                 start=(kc == 0), stop=(kc == KC - 1)
                                 ).then_inc(tv.h, 1)
                        tv.n += 1
                        pvd[kc] = tv.n
                        if kc + 2 < KC:
                            emit_qk(t, kc + 2)

                    if kc == 8 and deferred_bc[0] is not None:
                        deferred_bc[0]()
                        deferred_bc[0] = None

                @blk.scalar
                def _(s, p=p, nq4=nq4, pv15_prev=pv15_prev):
                    for kc in range(KC):
                        bA = 2 * (kc % 2)
                        s.wait_ge(tv.h, qkd[kc])
                        if kc >= 4:
                            s.wait_ge(tv.h, pvd[kc - 4])
                        elif kc == 0 and pv15_prev:
                            s.wait_ge(tv.h, pv15_prev)
                        sl = expab2[:, (kc % 4) // 2, 1024 * (kc % 2):1024 * (kc % 2) + 1024]
                        s.raw.activation(sl,
                                         PS[:, bA:bA + 2, :].rearrange("p a b -> p (a b)"),
                                         AF.Exp, bias=bias_m[:, 0:1]).then_inc(sv.h, 1)
                        sv.n += 1
                assert sv.n == sv_base + KC

                prev_exp_tail[0] = expd[KC - 1]
                prev_pv15[0] = pvd[KC - 1]

                @blk.vector
                def _(v, pv_end=pvd[KC - 1], pva=pva):
                    v.wait_ge(tv.h, pv_end)
                    v.tensor_copy(pvstage[0:65, 0, :], PS[0:65, pva, :])
                    v.tensor_copy(pvstage[:, 1, :], PS[:, pva + 1, :]).then_inc(vv.h, 1)
                    vv.n += 1
                stage_done = vv.n

                @blk.vector
                def _(v):
                    v.reciprocal(pvstage[64:65, 0, :], pvstage[64:65, 0, :])
                    v.reciprocal(pvstage[0:1, 1, :], pvstage[0:1, 1, :])
                    v.tensor_copy(recT[64:65, :], pvstage[64:65, 0, :])
                    v.tensor_copy(recT[0:1, :], pvstage[0:1, 1, :]).then_inc(vv.h, 1)
                    vv.n += 1
                rec_done = vv.n
                last_rec_done[0] = rec_done

                def emit_bcast_norm(p=p, q0=q0, rec_done=rec_done, pva=pva):
                    @blk.tensor
                    def _(t):
                        t.wait_ge(vv.h, rec_done)
                        t.wait_ge(vv.h, last_bias_done[0])
                        t.matmul(PS[:, pva, :], ones2[64:65, 0:128], recT[64:65, :],
                                 start=True, stop=True)
                        t.matmul(PS[:, pva + 1, :], ones2[0:1, 0:128], recT[0:1, :],
                                 start=True, stop=True).then_inc(tv.h, 1)
                        tv.n += 1
                    bcast_t = tv.n

                    @blk.vector
                    def _(v, bcast_t=bcast_t):
                        v.wait_ge(tv.h, bcast_t)
                        v.tensor_tensor(attn_n[0:64, p, q0:q0 + 512],
                                        pvstage[0:64, 0, :], PS[0:64, pva, :], ALU.mult)
                        v.tensor_tensor(attn_n[64:128, p, q0:q0 + 512],
                                        pvstage[64:128, 1, :], PS[64:128, pva + 1, :],
                                        ALU.mult).then_inc(vv.h, 1)
                        vv.n += 1
                    prev_norm_done[0] = vv.n
                deferred_bc[0] = emit_bcast_norm

        deferred_bc[0]()
        deferred_bc[0] = None
        attn_done = prev_norm_done[0]

        es2.close()
        esAI.close()

        es3 = ExitStack()
        S3 = lambda n, sh, dt: es3.enter_context(nc.sbuf_tensor(n, sh, dt))
        wo_sb = S3("wo_sb", [128, FC, D], BF16)
        wo_fe = S3("wo_fe", [128, 2, S], BF16)
        wo_tok = S3("wo_tok", [128, TC, D], BF16)
        ga_bc = S3("ga_bc", [128, D], F32)
        grs = S3("grs", [128, D], F32)

        @blk.gpsimd
        def _(g):
            g.wait_ge(vv.h, attn_done)
            for dc in range(FC):
                g.dma_start(wo_sb[:, dc, :], wo_h[dc * 128:(dc + 1) * 128, :]
                            ).then_inc(lg.h, 16)
            g.dma_start(ga_bc[:, :], scr[6:7, 0:D].partition_broadcast(128)
                        ).then_inc(lg.h, 16)
        lg.n += 16 * 9
        wo_ld = lg.n

        wo_mm_hist = []
        wo_ev_hist = []
        wo_dmat_hist = []
        for oc in range(FC):
            for nq in range(NQ):
                i = oc * NQ + nq

                @blk.tensor
                def _(t, oc=oc, nq=nq, i=i):
                    if i == 0:
                        t.wait_ge(lg.h, wo_ld)
                        t.wait_ge(vv.h, attn_done)
                    if i >= 2:
                        t.wait_ge(vv.h, wo_ev_hist[i - 2])
                    for dc in range(FC):
                        ins = t.matmul(PS[:, i % 2, :], wo_sb[:, dc, oc * 128:(oc + 1) * 128],
                                       attn_n[:, dc, nq * 512:(nq + 1) * 512],
                                       start=(dc == 0), stop=(dc == FC - 1))
                    ins.then_inc(tv.h, 1)
                    tv.n += 1
                wo_mm_hist.append(tv.n)

                @blk.vector
                def _(v, oc=oc, nq=nq, i=i):
                    v.wait_ge(tv.h, wo_mm_hist[i])
                    if oc >= 2 and nq == 0:
                        ds_, dv_ = wo_dmat_hist[oc - 2]
                        v.wait_ge(ds_.h, dv_)
                    v.tensor_copy(wo_fe[:, oc % 2, nq * 512:(nq + 1) * 512],
                                  PS[:, i % 2, :]).then_inc(vv.h, 1)
                    vv.n += 1
                wo_ev_hist.append(vv.n)

            dsem = dts if oc % 2 == 0 else dt1

            @blk.sync
            def _(sy, oc=oc, dsem=dsem):
                sy.wait_ge(vv.h, wo_ev_hist[oc * NQ + NQ - 1])
                sy.dma_start_transpose(wo_tok[:, :, oc * 128:(oc + 1) * 128],
                                       wo_fe[:, oc % 2, :]).then_inc(dsem.h, 16)
            dsem.n += 16
            wo_dmat_hist.append((dsem, dsem.n))
        wo_t_done = [(dts, dts.n), (dt1, dt1.n)]

        res1_hist = []
        for tc in range(TC):
            @blk.vector
            def _(v, tc=tc):
                if tc == 0:
                    for ds_, dv_ in wo_t_done:
                        v.wait_ge(ds_.h, dv_)
                v.tensor_tensor(grs[:, :], wo_tok[:, tc, :], ga_bc[:, :], ALU.mult)
                v.tensor_tensor(x_tok[:, tc, :], x_tok[:, tc, :], grs[:, :], ALU.add
                                ).then_inc(vv.h, 1)
                vv.n += 1
            res1_hist.append(vv.n)
        res1_done = vv.n

        es3.close()
        esL2.close()

        es4 = ExitStack()
        S4 = lambda n, sh, dt: es4.enter_context(nc.sbuf_tensor(n, sh, dt))
        cb1_bc = S4("cb1_bc", [128, D], F32)
        sbv_bc = S4("sbv_bc", [128, D], F32)
        gb_bc = S4("gb_bc", [128, D], BF16)
        gbias_bc = S4("gbias_bc", [128, D], BF16)
        h2 = S4("h2", [128, D], F32)
        h2B = S4("h2B", [128, D], F32)
        tq = S4("tq", [128, D], F32)
        tqB = S4("tqB", [128, D], F32)
        dq = S4("dq", [128, D], F32)
        dqB = S4("dqB", [128, D], F32)
        sg1 = S4("sg1", [128, D], BF16)
        sg1B = S4("sg1B", [128, D], BF16)
        sg2 = S4("sg2", [128, D], BF16)
        sg2B = S4("sg2B", [128, D], BF16)
        q8s = S4("q8s", [128, D], I8)
        q8sB = S4("q8sB", [128, D], I8)
        mcolB = S4("mcolB", [128, 2], F32)
        q8tok = S4("q8tok", [128, 2, D], BF16)
        q8fe = S4("q8fe", [128, FC, QH], BF16)
        ffn_fe = S4("ffn_fe", [128, 2, QH], BF16)
        ffn_tok = S4("ffn_tok", [128, 8, D], BF16)
        grs2 = S4("grs2", [128, D], F32)

        @blk.gpsimd
        def _(g):
            g.wait_ge(vv.h, res1_done)
            g.dma_start(cb1_bc[:, :], scr[4:5, 0:D].partition_broadcast(128)).then_inc(lg.h, 16)
            g.dma_start(sbv_bc[:, :], scr[5:6, 0:D].partition_broadcast(128)).then_inc(lg.h, 16)
            g.dma_start(gb_bc[:, :], scr[7:8, 0:D].partition_broadcast(128)).then_inc(lg.h, 16)
            g.dma_start(gbias_bc[:, :], scr[8:9, 0:D].partition_broadcast(128)).then_inc(lg.h, 16)
        lg.n += 64
        p6_bc = lg.n

        @blk.vector
        def _(v):
            v.wait_ge(vv.h, res1_done)
            for tc in range(TC):
                v.tensor_scalar(h2[:, :], x_tok[:, tc, :], 0.0, 0.0, ALU.add, ALU.add,
                                accum_out=stx[:, tc:tc + 1])
                v.scalar_tensor_tensor(h2[:, :], x_tok[:, tc, :], 1.0, x_tok[:, tc, :],
                                       ALU.mult, ALU.mult, accum_out=stxx[:, tc:tc + 1])
            v.drain()
            v.tensor_scalar(mu3[:, :], stx[:, :], 1.0 / D, None, ALU.mult)
            v.drain()
            v.tensor_tensor(asb[:, :], mu3[:, :], mu3[:, :], ALU.mult)
            v.drain()
            v.scalar_tensor_tensor(rasb[:, :], stxx[:, :], 1.0 / D, asb[:, :],
                                   ALU.mult, ALU.subtract)
            v.drain()
            v.engine_nop().then_inc(vv.h, 1)
            vv.n += 1
        var3_done = vv.n

        @blk.scalar
        def _(s):
            s.wait_ge(vv.h, var3_done)
            s.activation(asb[:, :], rasb[:, :], AF.Ln, bias=sc1[:, 0:1])
            s.activation(r3c[:, :], asb[:, :], AF.Exp, scale=-0.5).then_inc(sv.h, 1)
            sv.n += 1
        r3_done = sv.n

        q8_dmat_hist = []
        q8i_hist = []
        ep_hist = []
        store_hist = []
        h2p, tqp, dqp = [h2, h2B], [tq, tqB], [dq, dqB]
        sg1p, sg2p, q8sp, mcolp = [sg1, sg1B], [sg2, sg2B], [q8s, q8sB], [mcol, mcolB]

        def emit_q_head(tc):
            P_ = tc % 2

            @blk.vector
            def _(v, tc=tc, P_=P_):
                if tc == 0:
                    v.wait_ge(sv.h, r3_done)
                    v.wait_ge(lg.h, p6_bc)
                if len(q8i_hist) >= 2:
                    v.wait_ge(sv.h, q8i_hist[-2])
                v.tensor_scalar(h2p[P_][:, :], x_tok[:, tc, :], mu3[:, tc:tc + 1],
                                r3c[:, tc:tc + 1], ALU.subtract, ALU.mult)
                v.tensor_tensor(h2p[P_][:, :], h2p[P_][:, :], cb1_bc[:, :], ALU.mult)
                v.tensor_tensor(h2p[P_][:, :], h2p[P_][:, :], sbv_bc[:, :], ALU.add)
                v.reduce_max(mcolp[P_][:, 0:1], h2p[P_][:, :], AX.X,
                             apply_absolute_value=True)
                v.drain()
                v.tensor_scalar_max(mcolp[P_][:, 0:1], mcolp[P_][:, 0:1], 1e-5)
                v.drain()
                v.reciprocal(mcolp[P_][:, 1:2], mcolp[P_][:, 0:1])
                v.drain()
                v.tensor_scalar(asb[:, tc:tc + 1], mcolp[P_][:, 1:2], 127.0, None,
                                ALU.mult)
                v.tensor_scalar(rasb[:, tc:tc + 1], mcolp[P_][:, 0:1], wsb[:, 0:1],
                                1.0 / 127.0, ALU.mult, ALU.mult)
                v.drain()
                v.engine_nop().then_inc(vv.h, 1)
                vv.n += 1
            h2_done = vv.n

            @blk.scalar
            def _(s, tc=tc, P_=P_, h2_done=h2_done):
                s.wait_ge(vv.h, h2_done)
                s.activation(tqp[P_][:, :], h2p[P_][:, :], AF.Copy,
                             scale=asb[:, tc:tc + 1])
                s.activation(q8sp[P_][:, :], h2p[P_][:, :], AF.Copy,
                             scale=asb[:, tc:tc + 1]).then_inc(sv.h, 1)
                sv.n += 1
            q8i_hist.append(sv.n)

        def emit_q_tail(tc, tci, q8i_done):
            P_ = tc % 2

            @blk.vector
            def _(v, P_=P_, q8i_done=q8i_done):
                v.wait_ge(sv.h, q8i_done)
                v.tensor_tensor(dqp[P_][:, :], tqp[P_][:, :], q8sp[P_][:, :],
                                ALU.subtract).then_inc(vv.h, 1)
                vv.n += 1
            d_done = vv.n

            @blk.scalar
            def _(s, P_=P_, d_done=d_done):
                s.wait_ge(vv.h, d_done)
                s.activation(sg1p[P_][:, :], dqp[P_][:, :], AF.Sign, bias=schalf[:, 0:1])
                s.activation(sg2p[P_][:, :], dqp[P_][:, :], AF.Sign, bias=schalf[:, 1:2]
                             ).then_inc(sv.h, 1)
                sv.n += 1
            sgn_done = sv.n

            @blk.vector
            def _(v, tc=tc, P_=P_, sgn_done=sgn_done):
                v.wait_ge(sv.h, sgn_done)
                if len(q8_dmat_hist) >= 2:
                    ds_, dv_ = q8_dmat_hist[-2]
                    v.wait_ge(ds_.h, dv_)
                v.tensor_tensor(sg1p[P_][:, :], sg1p[P_][:, :], sg2p[P_][:, :], ALU.add)
                v.scalar_tensor_tensor(q8tok[:, tc % 2, :], sg1p[P_][:, :], 0.5,
                                       q8sp[P_][:, :], ALU.mult, ALU.add
                                       ).then_inc(vv.h, 1)
                vv.n += 1
            q8_done = vv.n

            dsem = dts if tc % 2 == 0 else dt1

            @blk.sync
            def _(sy, tc=tc, tci=tci, q8_done=q8_done, dsem=dsem):
                sy.wait_ge(vv.h, q8_done)
                sy.dma_start_transpose(q8fe[:, :, tci * 128:(tci + 1) * 128],
                                       q8tok[:, tc % 2, :]).then_inc(dsem.h, 16)
            dsem.n += 16
            q8_dmat_hist.append((dsem, dsem.n))

        for rg in range(2):
            for tci in range(8):
                tc = rg * 8 + tci
                emit_q_head(tc)
                if tci >= 1:
                    emit_q_tail(tc - 1, tci - 1, q8i_hist[-2])
            emit_q_tail(rg * 8 + 7, 7, q8i_hist[-1])
            q8_all = [(dts, dts.n), (dt1, dt1.n)]

            f_mm_hist = []
            f_ev_hist = []
            f_dmat_hist = []
            for oc in range(FC):
                for n2 in range(2):
                    i = oc * 2 + n2

                    @blk.tensor
                    def _(t, rg=rg, oc=oc, n2=n2, i=i, q8_all=q8_all):
                        if i == 0:
                            for ds_, dv_ in q8_all:
                                t.wait_ge(ds_.h, dv_)
                        if i >= 2:
                            t.wait_ge(vv.h, f_ev_hist[i - 2])
                        for fc in range(FC):
                            ins = t.matmul(PS[:, i % 2, :], wt_ffn[:, fc, oc * 128:(oc + 1) * 128],
                                           q8fe[:, fc, n2 * 512:(n2 + 1) * 512],
                                           start=(fc == 0), stop=(fc == FC - 1))
                        ins.then_inc(tv.h, 1)
                        tv.n += 1
                    f_mm_hist.append(tv.n)

                    @blk.vector
                    def _(v, oc=oc, n2=n2, i=i):
                        v.wait_ge(tv.h, f_mm_hist[i])
                        if oc >= 2 and n2 == 0:
                            ds_, dv_ = f_dmat_hist[oc - 2]
                            v.wait_ge(ds_.h, dv_)
                        v.tensor_copy(ffn_fe[:, oc % 2, n2 * 512:(n2 + 1) * 512],
                                      PS[:, i % 2, :]).then_inc(vv.h, 1)
                        vv.n += 1
                    f_ev_hist.append(vv.n)

                dsem = dts if oc % 2 == 0 else dt1

                @blk.sync
                def _(sy, oc=oc, dsem=dsem):
                    sy.wait_ge(vv.h, f_ev_hist[oc * 2 + 1])
                    sy.dma_start_transpose(ffn_tok[:, :, oc * 128:(oc + 1) * 128],
                                           ffn_fe[:, oc % 2, :]).then_inc(dsem.h, 16)
                dsem.n += 16
                f_dmat_hist.append((dsem, dsem.n))
            f_t_done = [(dts, dts.n), (dt1, dt1.n)]

            for tci in range(8):
                tc = rg * 8 + tci

                @blk.vector
                def _(v, tc=tc, tci=tci, f_t_done=f_t_done):
                    if tci == 0:
                        for ds_, dv_ in f_t_done:
                            v.wait_ge(ds_.h, dv_)
                    v.tensor_scalar(sg1[:, :], ffn_tok[:, tci, :], rasb[:, tc:tc + 1],
                                    None, ALU.mult)
                    v.tensor_tensor(sg2[:, :], sg1[:, :], gb_bc[:, :], ALU.mult)
                    v.tensor_tensor(grs2[:, :], sg2[:, :], gbias_bc[:, :], ALU.add)
                    v.tensor_tensor(x_tok[:, tc, :], x_tok[:, tc, :], grs2[:, :], ALU.add
                                    ).then_inc(vv.h, 1)
                    vv.n += 1
                ep_hist.append(vv.n)

                @blk.sync
                def _(sy, tc=tc):
                    sy.wait_ge(vv.h, ep_hist[-1])
                    sy.dma_start(out_h[tc * 128:(tc + 1) * 128, :], x_tok[:, tc, :]
                                 ).then_inc(st.h, 16)
                    st.n += 16
                store_hist.append(st.n)

        @blk.sync
        def _(sy):
            sy.wait_ge(st.h, st.n)

        es4.close()
    return nc


_NC_CACHE = {}


def kernel(L0_attn_latents, L0_attn_ln_gamma, L0_attn_ln_beta, L0_attn_Wq,
           L0_attn_Wk, L0_attn_Wv, L0_attn_Wo, L0_ffn_packed_weight,
           L0_ffn_weight_scale, L0_ffn_bias, L0_c, L0_wmod):
    if "nc" not in _NC_CACHE:
        _NC_CACHE["nc"] = build_nc()
    nc = _NC_CACHE["nc"]
    x = np.ascontiguousarray(np.asarray(L0_attn_latents, dtype=np.float32))
    shared = {
        "ln_g": np.asarray(L0_attn_ln_gamma, np.float32).reshape(1, D),
        "ln_b": np.asarray(L0_attn_ln_beta, np.float32).reshape(1, D),
        "wq": np.ascontiguousarray(np.asarray(L0_attn_Wq, np.float32)),
        "wk": np.ascontiguousarray(np.asarray(L0_attn_Wk, np.float32)),
        "wv": np.ascontiguousarray(np.asarray(L0_attn_Wv, np.float32)),
        "wo": np.ascontiguousarray(np.asarray(L0_attn_Wo, np.float32)),
        "pk": np.ascontiguousarray(np.asarray(L0_ffn_packed_weight, np.int32
                                              ).reshape(D // 4, D)),
        "wsc": np.asarray(L0_ffn_weight_scale, np.float32).reshape(1, 1),
        "fbias": np.asarray(L0_ffn_bias, np.float32).reshape(1, D),
        "wmod": np.ascontiguousarray(np.asarray(L0_wmod, np.float32)),
    }
    in_maps = []
    for b in range(8):
        m = dict(shared)
        m["x"] = np.ascontiguousarray(x[b])
        m["cvec"] = np.ascontiguousarray(np.asarray(L0_c, np.float32)[b].reshape(1, D))
        in_maps.append(m)
    res = run_bass_kernel_spmd(nc, in_maps, core_ids=list(range(8)))
    return np.stack([res.results[b]["out"] for b in range(8)], axis=0)


# version 23
# speedup vs baseline: 1.0077x; 1.0077x over previous
import numpy as np
import ml_dtypes
from contextlib import ExitStack
import concourse.bass as bass
import concourse.mybir as mybir
from concourse.bass_utils import run_bass_kernel_spmd

F32 = mybir.dt.float32
BF16 = mybir.dt.bfloat16
I32 = mybir.dt.int32
I8 = mybir.dt.int8
AF = mybir.ActivationFunctionType
ALU = mybir.AluOpType
AX = mybir.AxisListType

D = 1024
S = 2048
TC = S // 128
FC = D // 128
NP = 8
QH = 1024
NQ = 4
KC = 16
EPS = 1e-5
QKS = float(1.0 / np.sqrt(np.sqrt(64.0)))


class Sem:
    def __init__(self, es, nc, name):
        self.h = es.enter_context(nc.semaphore(name))
        self.n = 0

    def inc(self, instr):
        self.n += 1
        return instr

    def dinc(self, instr):
        self.n += 16
        return instr


def _ap_tname(a):
    t = getattr(a, "tensor", None)
    return getattr(t, "name", None) if t is not None else None


class EngP:

    def __init__(self, eng, acc):
        self.__dict__["_e"] = eng
        self.__dict__["_acc"] = acc

    @property
    def raw(self):
        return self._e

    def drain(self):
        self._e.drain()
        self._acc.clear()

    def wait_ge(self, *a, **k):
        return self._e.wait_ge(*a, **k)

    def __getattr__(self, name):
        f = getattr(self._e, name)
        if not callable(f):
            return f
        acc = self._acc
        eng = self._e

        def wrap(*args, **kwargs):
            names = set()
            for a in list(args) + list(kwargs.values()):
                n = _ap_tname(a)
                if n is not None:
                    names.add(n)
            if names & acc:
                eng.drain()
                acc.clear()
            acc.update(names)
            return f(*args, **kwargs)
        return wrap


class BlkWrap:
    def __init__(self, blk, dirty):
        self._blk = blk
        self._dirty = dirty

    @property
    def vector(self):
        def deco(body):
            @self._blk.vector
            def _(v):
                body(EngP(v, self._dirty["v"]))
            return _
        return deco

    @property
    def scalar(self):
        def deco(body):
            @self._blk.scalar
            def _(s):
                body(EngP(s, self._dirty["s"]))
            return _
        return deco

    def __getattr__(self, name):
        return getattr(self._blk, name)


def build_nc():
    nc = bass.Bass("TRN2", target_bir_lowering=False)

    x_h = nc.dram_tensor("x", [S, D], F32, kind="ExternalInput").ap()
    g_h = nc.dram_tensor("ln_g", [1, D], F32, kind="ExternalInput").ap()
    be_h = nc.dram_tensor("ln_b", [1, D], F32, kind="ExternalInput").ap()
    wq_h = nc.dram_tensor("wq", [D, D], F32, kind="ExternalInput").ap()
    wk_h = nc.dram_tensor("wk", [D, D], F32, kind="ExternalInput").ap()
    wv_h = nc.dram_tensor("wv", [D, D], F32, kind="ExternalInput").ap()
    wo_h = nc.dram_tensor("wo", [D, D], F32, kind="ExternalInput").ap()
    pk_h = nc.dram_tensor("pk", [D // 4, D], I32, kind="ExternalInput").ap()
    ws_h = nc.dram_tensor("wsc", [1, 1], F32, kind="ExternalInput").ap()
    fb_h = nc.dram_tensor("fbias", [1, D], F32, kind="ExternalInput").ap()
    c_h = nc.dram_tensor("cvec", [1, D], F32, kind="ExternalInput").ap()
    wm_h = nc.dram_tensor("wmod", [D, 6 * D], F32, kind="ExternalInput").ap()
    out_h = nc.dram_tensor("out", [S, D], F32, kind="ExternalOutput").ap()
    scr = nc.dram_tensor("scr_rows", [12, S], F32, kind="Internal").ap()

    es = ExitStack()
    SB = lambda n, sh, dt: es.enter_context(nc.sbuf_tensor(n, sh, dt))

    x_tok = SB("x_tok", [128, TC, D], F32)
    PS = es.enter_context(nc.psum_tensor("PS", [128, 8, 512], F32))
    sc1 = SB("sc1", [128, 1], F32)
    schalf = SB("schalf", [128, 2], F32)
    identf = SB("identf", [1, 1], F32)
    identb = SB("identb", [1, 1], BF16)
    ident128 = SB("ident128", [128, 128], BF16)
    ij128 = SB("ij128", [128, 128], I32)
    onesr = SB("onesr", [1, 128], BF16)
    scT = SB("scT", [128, 8], BF16)
    hsel = SB("hsel", [128, 2], BF16)
    bias_m = SB("bias_m", [128, 2], F32)
    nq2t = SB("nq2t", [2, 2], F32)
    negm = SB("negm", [2, 1], F32)
    negmT = SB("negmT", [1, 2], BF16)
    identf2 = SB("identf2", [2, 2], F32)
    wsb = SB("wsb", [128, 1], F32)
    scal = SB("scal", [1, 16], F32)
    eps1 = SB("eps1", [1, 1], F32)
    mu3 = SB("mu3", [128, TC], F32)
    r3c = SB("r3c", [128, TC], F32)
    asb = SB("asb", [128, TC], F32)
    rasb = SB("rasb", [128, TC], F32)
    stx = SB("stx", [128, TC], F32)
    stxx = SB("stxx", [128, TC], F32)
    mcol = SB("mcol", [128, 2], F32)
    wt_ffn = SB("wt_ffn", [128, FC, D], BF16)

    sems = {}
    for s_ in ("ld", "lg", "dt", "tv", "vv", "sv", "st", "wr", "ig",
               "bc", "pk", "d1"):
        sems[s_] = Sem(es, nc, "sm_" + s_)
    ld, lg, dts = sems["ld"], sems["lg"], sems["dt"]
    tv, vv, sv = sems["tv"], sems["vv"], sems["sv"]
    st, wr, ig = sems["st"], sems["wr"], sems["ig"]
    bc, pk, dt1 = sems["bc"], sems["pk"], sems["d1"]

    blk = BlkWrap(es.enter_context(nc.Block()), {"v": set(), "s": set()})

    def r3d(ap, w=512):
        return ap.rearrange("p (a b) -> p a b", b=w)

    with es:
        @blk.sync
        def _(sy):
            for tc in range(TC):
                sy.dma_start(x_tok[:, tc, :], x_h[tc * 128:(tc + 1) * 128, :]
                             ).then_inc(ld.h, 16)
                ld.n += 16
        x_loaded = ld.n

        esAI = ExitStack()
        ai = esAI.enter_context(nc.sbuf_tensor("ai", [128, FC, S], BF16, side="right"))
        es1 = ExitStack()
        S1 = lambda n, sh, dt: es1.enter_context(nc.sbuf_tensor(n, sh, dt))
        pool = S1("pool", [1, 10, 512], F32)
        uab = S1("uab", [1, S], BF16)
        agb = S1("agb", [1, D], BF16)
        Blhs = S1("Blhs", [4, D], BF16)
        Brhs = S1("Brhs", [4, S], BF16)
        statx = S1("statx", [128, FC, 4], BF16)
        statxx = S1("statxx", [128, FC, 2], BF16)
        es1a = ExitStack()
        S1A = lambda n, sh, dt: es1a.enter_context(nc.sbuf_tensor(n, sh, dt))
        vrow = S1A("vrow", [1, 12, D], BF16)
        frow = S1A("frow", [1, 4, D], F32)
        rowf = S1A("rowf", [1, D], F32)
        wmbuf = S1A("wmbuf", [128, 2, 3072], BF16)

        @blk.gpsimd
        def _(g):
            g.dma_start(wsb[:, 0:1], ws_h.partition_broadcast(128)).then_inc(lg.h, 16)
            g.dma_start(frow[0:1, 0, :], c_h).then_inc(lg.h, 16)
            g.dma_start(frow[0:1, 2, :], g_h).then_inc(lg.h, 16)
            g.dma_start(frow[0:1, 3, :], be_h).then_inc(lg.h, 16)
            g.dma_start(vrow[0:1, 11, :], fb_h).then_inc(lg.h, 16)
            g.iota(ij128[:], [[1, 128]], base=0, channel_multiplier=-1).then_inc(ig.h, 1)
        lg.n += 80
        ig.n += 1
        base_ld = lg.n

        @blk.vector
        def _(v):
            v.memset(sc1[:], EPS)
            v.memset(eps1[:], EPS)
            v.memset(schalf[:, 0:1], -0.5)
            v.memset(schalf[:, 1:2], 0.5)
            v.memset(identf[:], 1.0)
            v.memset(identb[:], 1.0)
            v.memset(onesr[:], 1.0)
            v.memset(hsel[:, 0:1], 1.0)
            v.memset(hsel[:, 1:2], 0.0)
            v.memset(statx[:, :, 0:1], 1.0)
            v.memset(statxx[:, :, 0:1], 1.0)
            v.wait_ge(lg.h, base_ld)
            v.wait_ge(ig.h, 1)
            v.tensor_scalar(ident128[:], ij128[:], 0, None, ALU.is_equal)
            v.tensor_scalar(identf2[:], ij128[0:2, 0:2], 0, None, ALU.is_equal
                            ).then_inc(vv.h, 1)
            vv.n += 1
        consts_done = vv.n

        @blk.scalar
        def _(s):
            s.wait_ge(lg.h, base_ld)
            s.activation(frow[0:1, 1, :], frow[0:1, 0, :], AF.Silu
                         ).then_inc(sv.h, 1)
            sv.n += 1
        silu_done = sv.n

        @blk.tensor
        def _(t):
            t.wait_ge(sv.h, silu_done)
            t.wait_ge(vv.h, consts_done)
            for k in range(8):
                ins = t.transpose(PS[:, 6, k:k + 1], frow[0:1, 1, k * 128:(k + 1) * 128],
                                  identf[0:1, 0:1])
            ins.then_inc(tv.h, 1)
            tv.n += 1
        scT_tp = tv.n

        @blk.vector
        def _(v):
            v.wait_ge(tv.h, scT_tp)
            v.tensor_copy(scT[:, 0:8], PS[:, 6, 0:8]).then_inc(vv.h, 1)
            vv.n += 1
        scT_done = vv.n

        mod_mm_hist = []
        mod_evac = [scT_done]
        chunk_ld = []
        for hf in range(2):
            for kc in range(FC):
                i = hf * FC + kc

                psem = lg if i % 2 == 0 else bc

                @blk.gpsimd
                def _(g, hf=hf, kc=kc, i=i, psem=psem):
                    if i == 0:
                        g.wait_ge(lg.h, base_ld)
                    if i >= 2:
                        g.wait_ge(tv.h, mod_mm_hist[i - 2])
                    g.dma_start(wmbuf[:, i % 2, :],
                                wm_h[kc * 128:(kc + 1) * 128,
                                     hf * 3072:(hf + 1) * 3072]
                                ).then_inc(psem.h, 16)
                psem.n += 16
                chunk_ld.append((psem, psem.n))

                @blk.tensor
                def _(t, hf=hf, kc=kc, i=i):
                    sem_i, cnt_i = chunk_ld[i]
                    t.wait_ge(sem_i.h, cnt_i)
                    if kc == 0:
                        t.wait_ge(vv.h, mod_evac[-1])
                    for ncol in range(6):
                        ins = t.matmul(PS[0:1, ncol, :], scT[:, kc:kc + 1],
                                       wmbuf[:, kc % 2, ncol * 512:(ncol + 1) * 512],
                                       start=(kc == 0), stop=(kc == FC - 1))
                    ins.then_inc(tv.h, 1)
                    tv.n += 1
                mod_mm_hist.append(tv.n)

            @blk.vector
            def _(v, hf=hf):
                v.wait_ge(tv.h, mod_mm_hist[-1])
                if hf == 0:
                    for j in range(3):
                        ins = v.tensor_copy(r3d(vrow[0:1, j, :]), PS[0:1, 2 * j:2 * j + 2, :])
                else:
                    v.tensor_copy(r3d(rowf[0:1, :]), PS[0:1, 0:2, :])
                    ins = v.tensor_copy(r3d(vrow[0:1, 5, :]), PS[0:1, 4:6, :])
                ins.then_inc(vv.h, 1)
                vv.n += 1
            mod_evac.append(vv.n)
        mod_h1_evac = mod_evac[-1]

        @blk.sync
        def _(sy):
            sy.wait_ge(vv.h, mod_h1_evac)
            sy.dma_start(scr[5:6, 0:D], rowf[0:1, :]).then_inc(wr.h, 16)
            wr.n += 16
        sb_written = wr.n

        @blk.vector
        def _(v):
            v.wait_ge(wr.h, sb_written)
            v.tensor_scalar(r3d(rowf[0:1, :]), PS[0:1, 2:4, :], 1.0, None, ALU.add
                            ).then_inc(vv.h, 1)
            vv.n += 1
        cb1_row = vv.n

        @blk.sync
        def _(sy):
            sy.wait_ge(vv.h, cb1_row)
            sy.dma_start(scr[4:5, 0:D], rowf[0:1, :]).then_inc(wr.h, 16)
            wr.n += 16
        cb1_written = wr.n

        @blk.vector
        def _(v):
            v.wait_ge(wr.h, cb1_written)
            v.tensor_scalar(vrow[0:1, 6, :], vrow[0:1, 1, :], 1.0, None, ALU.add)
            for (src, dst) in ((2, 7), (5, 8)):
                v.tensor_scalar(vrow[0:1, 9, :], vrow[0:1, src, :], 0.0, None, ALU.is_ge)
                v.tensor_scalar(vrow[0:1, 9, :], vrow[0:1, 9, :], 2.0, -1.0, ALU.mult, ALU.add)
                v.tensor_scalar(vrow[0:1, 10, :], vrow[0:1, src, :], -1.0, None, ALU.mult)
                v.tensor_tensor(vrow[0:1, 10, :], vrow[0:1, src, :], vrow[0:1, 10, :], ALU.max)
                v.tensor_scalar_max(vrow[0:1, 10, :], vrow[0:1, 10, :], 0.25)
                v.tensor_tensor(vrow[0:1, dst, :], vrow[0:1, 9, :], vrow[0:1, 10, :], ALU.mult)
            v.reduce_sum(scal[0:1, 0:1], vrow[0:1, 6, :], AX.X)
            v.tensor_tensor(vrow[0:1, 9, :], vrow[0:1, 6, :], vrow[0:1, 6, :], ALU.mult)
            v.reduce_sum(scal[0:1, 1:2], vrow[0:1, 9, :], AX.X)
            v.tensor_tensor(vrow[0:1, 10, :], vrow[0:1, 6, :], vrow[0:1, 0, :], ALU.mult)
            v.reduce_sum(scal[0:1, 2:3], vrow[0:1, 10, :], AX.X)
            v.tensor_tensor(vrow[0:1, 4, :], vrow[0:1, 0, :], vrow[0:1, 0, :], ALU.mult)
            v.reduce_sum(scal[0:1, 3:4], vrow[0:1, 4, :], AX.X)
            v.reduce_sum(scal[0:1, 4:5], vrow[0:1, 0, :], AX.X)
            v.drain()
            v.tensor_scalar(scal[0:1, 3:4], scal[0:1, 3:4], 1.0 / D, None, ALU.mult)
            v.tensor_scalar(scal[0:1, 4:5], scal[0:1, 4:5], 1.0 / D, None, ALU.mult
                            ).then_inc(vv.h, 1)
            vv.n += 1
        drows_done = vv.n

        @blk.tensor
        def _(t):
            t.wait_ge(vv.h, drows_done)
            psb = PS[:, 7, :].bitcast(BF16)
            for (j, r) in enumerate((6, 9, 10)):
                for k in range(8):
                    c = 2 * (j * 8 + k)
                    ins = t.transpose(psb[:, c:c + 1],
                                      vrow[0:1, r, k * 128:(k + 1) * 128], identb[0:1, 0:1])
            ins.then_inc(tv.h, 1)
            tv.n += 1
        stat_tp = tv.n

        @blk.vector
        def _(v):
            v.wait_ge(tv.h, stat_tp)
            psb = PS[:, 7, :].bitcast(BF16)
            v.tensor_copy(statx[:, :, 1:2], psb[:, 0:16:2].unsqueeze(2))
            v.tensor_copy(statx[:, :, 2:3], psb[:, 16:32:2].unsqueeze(2))
            v.tensor_copy(statx[:, :, 3:4], psb[:, 32:48:2].unsqueeze(2))
            v.tensor_copy(statxx[:, :, 1:2], psb[:, 16:32:2].unsqueeze(2)).then_inc(vv.h, 1)
            vv.n += 1
        stat_lhs_done = vv.n

        def row_to_scr(make_row, scr_row, width=D):
            @blk.vector
            def _(v):
                make_row(v)
                v.engine_nop().then_inc(vv.h, 1)
                vv.n += 1
            r_t = vv.n

            @blk.sync
            def _(sy, r_t=r_t, scr_row=scr_row, width=width):
                sy.wait_ge(vv.h, r_t)
                sy.dma_start(scr[scr_row:scr_row + 1, 0:width], rowf[0:1, 0:width]
                             ).then_inc(wr.h, 16)
                wr.n += 16
            return wr.n

        def _mk(slot):
            def f(v):
                v.wait_ge(wr.h, last_w[0])
                v.tensor_copy(rowf[0:1, :], vrow[0:1, slot, :])
            return f
        last_w = [cb1_written]
        last_w[0] = row_to_scr(_mk(7), 6)
        last_w[0] = row_to_scr(_mk(8), 7)

        def _mk_gbias(v):
            v.wait_ge(wr.h, last_w[0])
            v.tensor_tensor(rowf[0:1, :], vrow[0:1, 8, :], vrow[0:1, 11, :], ALU.mult)
        last_w[0] = row_to_scr(_mk_gbias, 8)

        def _mk_sg(v):
            v.wait_ge(wr.h, last_w[0])
            v.tensor_tensor(rowf[0:1, :], vrow[0:1, 0, :], frow[0:1, 2, :], ALU.mult)
        last_w[0] = row_to_scr(_mk_sg, 1)

        def _mk_ag(v):
            v.wait_ge(wr.h, last_w[0])
            v.tensor_tensor(rowf[0:1, :], vrow[0:1, 6, :], frow[0:1, 2, :], ALU.mult)
            v.tensor_copy(agb[0:1, :], rowf[0:1, :])
        last_w[0] = row_to_scr(_mk_ag, 2)

        @blk.sync
        def _(sy):
            sy.dma_start(scr[3:4, 0:D], frow[0:1, 2, :]).then_inc(wr.h, 16)
            sy.dma_start(scr[0:1, 0:D], frow[0:1, 3, :]).then_inc(wr.h, 16)
            wr.n += 32
        blhs_written = wr.n

        @blk.gpsimd
        def _(g):
            g.wait_ge(wr.h, blhs_written)
            g.dma_start(Blhs[0:4, 0:D], scr[0:4, 0:D]).then_inc(lg.h, 16)
        lg.n += 16
        blhs_ld = lg.n

        es1a.close()
        es1b = ExitStack()
        S1B = lambda n, sh, dt: es1b.enter_context(nc.sbuf_tensor(n, sh, dt))
        xfe = S1B("xfe", [128, FC, S], BF16)
        xbf = S1B("xbf", [128, 2, D], BF16)
        x2c = S1B("x2c", [128, 2, 512], BF16)
        ais = S1B("ais", [128, S], F32)
        dmat_hist = []
        for tc in range(TC):
            dsem_x = dts if tc % 2 == 0 else dt1

            @blk.vector
            def _(v, tc=tc):
                if tc == 0:
                    v.wait_ge(ld.h, x_loaded)
                    v.wait_ge(tv.h, stat_tp)
                    v.wait_ge(wr.h, blhs_written)
                    v.wait_ge(lg.h, blhs_ld)
                elif tc >= 2:
                    ds_, dv_ = dmat_hist[tc - 2]
                    v.wait_ge(ds_.h, dv_)
                v.tensor_copy(xbf[:, tc % 2, :], x_tok[:, tc, :]).then_inc(vv.h, 1)
                vv.n += 1
            xbf_c = vv.n

            @blk.sync
            def _(sy, tc=tc, xbf_c=xbf_c, dsem_x=dsem_x):
                sy.wait_ge(vv.h, xbf_c)
                sy.dma_start_transpose(xfe[:, :, tc * 128:(tc + 1) * 128],
                                       xbf[:, tc % 2, :]).then_inc(dsem_x.h, 16)
            dsem_x.n += 16
            dmat_hist.append((dsem_x, dsem_x.n))
        xfe_done = [(dts, dts.n), (dt1, dt1.n)]

        stat_ev_hist = []
        srow_hist = []
        P = lambda i: pool[0:1, i, :]
        SA, SAA, SAS = scal[0:1, 0:1], scal[0:1, 1:2], scal[0:1, 2:3]
        SSS_D, SS_D = scal[0:1, 3:4], scal[0:1, 4:5]
        for nq in range(NQ):
            q0 = nq * 512
            sq_hist = []
            mm_hist = []
            for fc in range(FC):
                @blk.vector
                def _(v, fc=fc, nq=nq, q0=q0):
                    if fc == 0:
                        for ds_, dv_ in (dmat_hist[nq * 4 + 2], dmat_hist[nq * 4 + 3]):
                            v.wait_ge(ds_.h, dv_)
                    if nq == 0 and fc == 0:
                        v.wait_ge(vv.h, stat_lhs_done)
                    if fc >= 2:
                        v.wait_ge(tv.h, mm_hist[fc - 2])
                    v.tensor_tensor(x2c[:, fc % 2, :], xfe[:, fc, q0:q0 + 512],
                                    xfe[:, fc, q0:q0 + 512], ALU.mult).then_inc(vv.h, 1)
                    vv.n += 1
                sq_hist.append(vv.n)

                @blk.tensor
                def _(t, fc=fc, nq=nq, q0=q0):
                    t.wait_ge(vv.h, sq_hist[fc])
                    if fc == 0 and nq > 0:
                        t.wait_ge(vv.h, stat_ev_hist[nq - 1])
                    for s_i in range(4):
                        t.matmul(PS[0:1, s_i, :], statx[:, fc, s_i:s_i + 1],
                                 xfe[:, fc, q0:q0 + 512],
                                 start=(fc == 0), stop=(fc == FC - 1))
                    for s_i in range(2):
                        ins = t.matmul(PS[0:1, 4 + s_i, :], statxx[:, fc, s_i:s_i + 1],
                                       x2c[:, fc % 2, :],
                                       start=(fc == 0), stop=(fc == FC - 1))
                    ins.then_inc(tv.h, 1)
                    tv.n += 1
                mm_hist.append(tv.n)
            stats_done = tv.n

            @blk.vector
            def _(v, stats_done=stats_done, nq=nq):
                v.wait_ge(tv.h, stats_done)
                if nq >= 1:
                    v.wait_ge(wr.h, srow_hist[nq - 1])
                for s_i in range(6):
                    ins = v.tensor_copy(P(s_i), PS[0:1, s_i, :])
                ins.then_inc(vv.h, 1)
                vv.n += 1
            stat_ev_hist.append(vv.n)

            @blk.vector
            def _(v):
                v.tensor_scalar(P(0), P(0), 1.0 / D, None, ALU.mult)
                v.tensor_tensor(P(6), P(0), P(0), ALU.mult)
                v.scalar_tensor_tensor(P(7), P(4), 1.0 / D, P(6), ALU.mult, ALU.subtract)
                v.engine_nop().then_inc(vv.h, 1)
                vv.n += 1
            alg1 = vv.n

            @blk.scalar
            def _(s, alg1=alg1):
                s.wait_ge(vv.h, alg1)
                s.activation(P(8), P(7), AF.Ln, bias=eps1[0:1, 0:1])
                s.activation(P(7), P(8), AF.Exp, scale=-0.5).then_inc(sv.h, 1)
                sv.n += 1
            alg_s1 = sv.n

            @blk.vector
            def _(v, alg_s1=alg_s1):
                v.wait_ge(sv.h, alg_s1)
                v.scalar_tensor_tensor(P(6), P(0), SA, P(1), ALU.mult, ALU.subtract)
                v.tensor_tensor(P(8), P(6), P(7), ALU.mult)
                v.tensor_scalar(P(8), P(8), -1.0 / D, SS_D, ALU.mult, ALU.add)
                v.scalar_tensor_tensor(P(6), P(0), SAA, P(2), ALU.mult, ALU.subtract)
                v.tensor_tensor(P(9), P(0), P(6), ALU.mult)
                v.tensor_tensor(P(6), P(0), P(2), ALU.mult)
                v.tensor_tensor(P(2), P(5), P(9), ALU.add)
                v.tensor_tensor(P(9), P(2), P(6), ALU.subtract)
                v.scalar_tensor_tensor(P(6), P(0), SAS, P(3), ALU.mult, ALU.subtract)
                v.tensor_tensor(P(2), P(7), P(6), ALU.mult)
                v.tensor_tensor(P(6), P(7), P(7), ALU.mult)
                v.tensor_tensor(P(3), P(6), P(9), ALU.mult)
                v.scalar_tensor_tensor(P(9), P(2), -2.0, P(3), ALU.mult, ALU.add)
                v.tensor_scalar(P(9), P(9), 1.0 / D, SSS_D, ALU.mult, ALU.add)
                v.tensor_tensor(P(6), P(8), P(8), ALU.mult)
                v.tensor_tensor(P(9), P(9), P(6), ALU.subtract)
                v.engine_nop().then_inc(vv.h, 1)
                vv.n += 1
            alg2 = vv.n

            @blk.scalar
            def _(s, alg2=alg2):
                s.wait_ge(vv.h, alg2)
                s.activation(P(6), P(9), AF.Ln, bias=eps1[0:1, 0:1])
                s.activation(P(9), P(6), AF.Exp, scale=-0.5).then_inc(sv.h, 1)
                sv.n += 1
            alg_s2 = sv.n

            @blk.vector
            def _(v, q0=q0, alg_s2=alg_s2):
                v.wait_ge(sv.h, alg_s2)
                v.tensor_tensor(P(6), P(7), P(9), ALU.mult)
                v.tensor_copy(uab[0:1, q0:q0 + 512], P(6))
                v.tensor_tensor(P(2), P(0), P(6), ALU.mult)
                v.tensor_scalar(P(2), P(2), -1.0, None, ALU.mult)
                v.tensor_tensor(P(3), P(8), P(9), ALU.mult)
                v.tensor_scalar(P(3), P(3), -1.0, None, ALU.mult)
                v.engine_nop().then_inc(vv.h, 1)
                vv.n += 1
            alg3 = vv.n

            @blk.sync
            def _(sy, q0=q0, alg3=alg3, nq=nq):
                sy.wait_ge(vv.h, alg3)
                if nq == 0:
                    sy.wait_ge(wr.h, blhs_written)
                sy.dma_start(scr[9:10, q0:q0 + 512], P(9)).then_inc(wr.h, 16)
                sy.dma_start(scr[10:11, q0:q0 + 512], P(2)).then_inc(wr.h, 16)
                sy.dma_start(scr[11:12, q0:q0 + 512], P(3)).then_inc(wr.h, 16)
                wr.n += 48
            srow_hist.append(wr.n)
        brhs_written = wr.n

        @blk.gpsimd
        def _(g):
            g.wait_ge(wr.h, brhs_written)
            g.dma_start(Brhs[1:4, 0:S], scr[9:12, 0:S]).then_inc(lg.h, 16)
        lg.n += 16
        brhs_ld = lg.n

        @blk.vector
        def _(v):
            v.wait_ge(lg.h, brhs_ld)
            v.memset(Brhs[0:1, :], 1.0).then_inc(vv.h, 1)
            vv.n += 1
        brhs_done = vv.n

        ab_hist = []
        ap_hist = []
        for fc in range(FC):
            @blk.tensor
            def _(t, fc=fc):
                if fc == 0:
                    t.wait_ge(vv.h, brhs_done)
                    t.wait_ge(lg.h, blhs_ld)
                else:
                    t.wait_ge(vv.h, ap_hist[fc - 1])
                for nq in range(NQ):
                    t.matmul(PS[:, nq, :], agb[0:1, fc * 128:(fc + 1) * 128],
                             uab[0:1, nq * 512:(nq + 1) * 512], start=True, stop=True)
                    ins = t.matmul(PS[:, 4 + nq, :], Blhs[0:4, fc * 128:(fc + 1) * 128],
                                   Brhs[0:4, nq * 512:(nq + 1) * 512], start=True, stop=True)
                ins.then_inc(tv.h, 1)
                tv.n += 1
            ab_hist.append(tv.n)

            @blk.vector
            def _(v, fc=fc):
                v.wait_ge(tv.h, ab_hist[fc])
                v.tensor_tensor(r3d(ais[:, :]), r3d(xfe[:, fc, :]), PS[:, 0:4, :], ALU.mult)
                v.tensor_tensor(r3d(ai[:, fc, :]), r3d(ais[:, :]), PS[:, 4:8, :], ALU.add
                                ).then_inc(vv.h, 1)
                vv.n += 1
            ap_hist.append(vv.n)
        ai_done = vv.n

        es1b.close()
        es1.close()

        esL2 = ExitStack()
        attn_n = esL2.enter_context(nc.sbuf_tensor("attn_n", [128, NP, S], BF16))
        es2 = ExitStack()
        S2 = lambda n, sh, dt: es2.enter_context(nc.sbuf_tensor(n, sh, dt))
        expab2 = S2("expab2", [128, 2, 2048], BF16)
        pvstage = S2("pvstage", [128, 2, 512], F32)
        recT = S2("recT", [128, 512], BF16)
        ones2 = S2("ones2", [128, 128], BF16)
        q_p = S2("q_p", [128, S], BF16)
        k_p = S2("k_p", [128, S], BF16)
        v_p = S2("v_p", [128, TC, 193], BF16)
        v_fe = S2("v_fe", [128, S], BF16)
        v_t = S2("v_t", [128, TC, 128], BF16)
        wst = S2("wst", [128, 3, FC, 128], BF16)
        pkb = S2("pkb", [128, 2, 1024], I32)
        encb = S2("encb", [128, D], I32)
        ternf = S2("ternf", [128, D], BF16)

        @blk.vector
        def _(v):
            v.drain()
            v.memset(v_p[:, :, 64:66], 1.0)
            v.memset(v_p[:, :, 66:129], 0.0)
            v.memset(ones2[0:1, :], 1.0)
            v.memset(ones2[64:65, :], 1.0).then_inc(vv.h, 1)
            vv.n += 1
        vp_init = vv.n

        pk_lds = []
        for rb in range(2):
            @blk.gpsimd
            def _(g, rb=rb):
                if rb == 0:
                    g.wait_ge(vv.h, ai_done)
                else:
                    g.wait_ge(pk.h, pk_lds[0])
                g.dma_start(pkb[:, rb, :], pk_h[rb * 128:(rb + 1) * 128, :]
                            ).then_inc(pk.h, 16)
            pk.n += 16
            pk_lds.append(pk.n)

        up_ev_hist = [ai_done]
        up_tp_hist = []
        for i in range(8):
            rb, s_i = i // 4, i % 4
            oc = 2 * s_i + rb

            @blk.vector
            def _(v, rb=rb, s_i=s_i, i=i):
                v.wait_ge(pk.h, pk_lds[rb])
                if i > 0:
                    v.wait_ge(tv.h, up_tp_hist[-1])
                v.tensor_scalar(encb[:, :], pkb[:, rb, :], 2 * s_i, 3,
                                ALU.logical_shift_right, ALU.bitwise_and)
                v.tensor_scalar(ternf[:, :], encb[:, :], 1.0, None, ALU.subtract
                                ).then_inc(vv.h, 1)
                vv.n += 1
            up_sq = vv.n

            @blk.tensor
            def _(t, i=i, up_sq=up_sq):
                t.wait_ge(vv.h, up_sq)
                if i == 0:
                    t.wait_ge(vv.h, ai_done)
                else:
                    t.wait_ge(vv.h, up_ev_hist[-1])
                psb = PS[:, 6:8, :].rearrange("p a b -> p (a b)").bitcast(BF16)
                for ic in range(FC):
                    ins = t.transpose(psb[:, ic * 128:(ic + 1) * 128],
                                      ternf[:, ic * 128:(ic + 1) * 128], ident128[:, :])
                ins.then_inc(tv.h, 1)
                tv.n += 1
            up_tp_hist.append(tv.n)
            up_tp = tv.n

            @blk.vector
            def _(v, oc=oc, up_tp=up_tp):
                v.wait_ge(tv.h, up_tp)
                psb = PS[:, 6:8, :].rearrange("p a b -> p (a b)").bitcast(BF16)
                v.tensor_copy(wt_ffn[:, :, oc * 128:(oc + 1) * 128],
                              psb[:, 0:1024].rearrange("p (a b) -> p a b", b=128)
                              ).then_inc(vv.h, 1)
                vv.n += 1
            up_ev_hist.append(vv.n)
        unpack_done = vv.n

        prev_proj_done = [0]
        prev_vp_ready = [0]
        prev_exp_tail = [0]
        prev_pv15 = [0]
        prev_norm_done = [0]
        last_rec_done = [0]
        last_bias_done = [0]
        deferred_bc = [None]
        for p in range(NP):
            @blk.gpsimd
            def _(g, p=p):
                if p == 0:
                    g.wait_ge(vv.h, ai_done)
                else:
                    g.wait_ge(tv.h, prev_proj_done[0])
                for (wi, wh) in ((0, wq_h), (1, wk_h), (2, wv_h)):
                    src = wh[:, p * 128:(p + 1) * 128]
                    for fc in range(FC):
                        g.dma_start(wst[:, wi, fc, :],
                                    src[fc * 128:(fc + 1) * 128, :]).then_inc(lg.h, 16)
            lg.n += 16 * 24
            wst_ld = lg.n

            qk_ev = []
            for (wi, dest) in ((0, q_p), (1, k_p)):
                @blk.tensor
                def _(t, p=p, wi=wi, wst_ld=wst_ld):
                    t.wait_ge(lg.h, wst_ld)
                    if p == 0 and wi == 0:
                        t.wait_ge(vv.h, unpack_done)
                        t.wait_ge(vv.h, ai_done)
                    if wi == 0 and p > 0:
                        t.wait_ge(sv.h, prev_exp_tail[0])
                    if wi == 1:
                        t.wait_ge(vv.h, qk_ev[0])
                    for nq in range(NQ):
                        for fc in range(FC):
                            ins = t.matmul(PS[:, nq, :], wst[:, wi, fc, :],
                                           ai[:, fc, nq * 512:(nq + 1) * 512],
                                           start=(fc == 0), stop=(fc == FC - 1))
                    ins.then_inc(tv.h, 1)
                    tv.n += 1
                mm_t = tv.n

                @blk.vector
                def _(v, dest=dest, mm_t=mm_t):
                    v.wait_ge(tv.h, mm_t)
                    v.tensor_scalar(r3d(dest[:, :]), PS[:, 0:4, :], QKS, None, ALU.mult
                                    ).then_inc(vv.h, 1)
                    vv.n += 1
                qk_ev.append(vv.n)
            k_ev = qk_ev[1]

            @blk.tensor
            def _(t, p=p):
                t.wait_ge(vv.h, qk_ev[1])
                for nq in range(NQ):
                    for fc in range(FC):
                        ins = t.matmul(PS[:, nq, :], wst[:, 2, fc, :],
                                       ai[:, fc, nq * 512:(nq + 1) * 512],
                                       start=(fc == 0), stop=(fc == FC - 1))
                ins.then_inc(tv.h, 1)
                tv.n += 1
            v_mm = tv.n

            @blk.vector
            def _(v, v_mm=v_mm):
                v.wait_ge(tv.h, v_mm)
                v.tensor_copy(r3d(v_fe[:, :]), PS[:, 0:4, :]).then_inc(vv.h, 1)
                vv.n += 1
            v_fe_done = vv.n

            @blk.sync
            def _(sy, v_fe_done=v_fe_done, p=p):
                sy.wait_ge(vv.h, v_fe_done)
                if p > 0:
                    sy.wait_ge(vv.h, prev_vp_ready[0])
                sy.dma_start_transpose(v_t[:, :, :], v_fe[:, :]).then_inc(dts.h, 16)
                dts.n += 16
            v_tp = dts.n

            proj_done = tv.n
            prev_proj_done[0] = proj_done

            nrm_mm = []
            sq_hist = []
            for (j, srcb) in ((0, q_p), (1, k_p)):
                for nq in range(NQ):
                    i2 = j * NQ + nq

                    @blk.vector
                    def _(v, j=j, srcb=srcb, nq=nq, i2=i2, k_ev=k_ev, p=p):
                        if i2 == 0:
                            v.wait_ge(vv.h, k_ev)
                            if p > 0:
                                v.wait_ge(tv.h, prev_pv15[0])
                        if i2 >= 3:
                            v.wait_ge(tv.h, nrm_mm[i2 - 3])
                        sl = expab2[:, (i2 % 4) // 2,
                                    512 * (i2 % 2):512 * (i2 % 2) + 512]
                        v.tensor_tensor(sl,
                                        srcb[:, nq * 512:(nq + 1) * 512],
                                        srcb[:, nq * 512:(nq + 1) * 512], ALU.mult
                                        ).then_inc(vv.h, 1)
                        vv.n += 1
                    sq_hist.append(vv.n)

                    @blk.tensor
                    def _(t, j=j, nq=nq, i2=i2):
                        t.wait_ge(vv.h, sq_hist[i2])
                        sl = expab2[:, (i2 % 4) // 2,
                                    512 * (i2 % 2):512 * (i2 % 2) + 512]
                        t.matmul(PS[0:1, nq, :], hsel[:, 0:1], sl,
                                 start=True, stop=True).then_inc(tv.h, 1)
                        tv.n += 1
                    nrm_mm.append(tv.n)

                @blk.vector
                def _(v, j=j):
                    v.wait_ge(tv.h, nrm_mm[j * NQ + NQ - 1])
                    v.reduce_max(nq2t[0:1, j:j + 1],
                                 PS[0:1, 0:4, :].rearrange("p a b -> p (a b)"), AX.X
                                 ).then_inc(vv.h, 1)
                    vv.n += 1
            @blk.vector
            def _(v):
                v.drain()
                v.tensor_tensor(negm[0:1, 0:1], nq2t[0:1, 0:1], nq2t[0:1, 1:2], ALU.mult
                                ).then_inc(vv.h, 1)
                vv.n += 1
            m2_d = vv.n

            @blk.scalar
            def _(s, m2_d=m2_d):
                s.wait_ge(vv.h, m2_d)
                s.activation(negm[0:1, 0:1], negm[0:1, 0:1], AF.Ln, bias=sc1[0:1, 0:1])
                s.activation(negm[0:1, 0:1], negm[0:1, 0:1], AF.Exp, scale=0.5
                             ).then_inc(sv.h, 1)
                sv.n += 1
            m_d = sv.n

            @blk.vector
            def _(v, m_d=m_d):
                v.wait_ge(sv.h, m_d)
                v.tensor_scalar(negmT[0:1, 0:1], negm[0:1, 0:1], -1.0, None, ALU.mult
                                ).then_inc(vv.h, 1)
                vv.n += 1
            negm_d = vv.n

            @blk.tensor
            def _(t, negm_d=negm_d, p=p):
                t.wait_ge(vv.h, negm_d)
                if p > 0:
                    t.wait_ge(vv.h, last_rec_done[0])
                t.matmul(PS[:, 7, 0:1], ones2[0:1, 0:128], negmT[0:1, 0:1],
                         start=True, stop=True).then_inc(tv.h, 1)
                tv.n += 1
            bb = tv.n

            @blk.vector
            def _(v, bb=bb):
                v.wait_ge(tv.h, bb)
                v.tensor_copy(bias_m[:, 0:1], PS[:, 7, 0:1]).then_inc(vv.h, 1)
                vv.n += 1
            bias_done = vv.n
            last_bias_done[0] = bias_done

            @blk.vector
            def _(v, v_tp=v_tp, p=p):
                v.wait_ge(dts.h, v_tp)
                if p == 0:
                    v.wait_ge(vv.h, vp_init)
                else:
                    v.wait_ge(tv.h, prev_pv15[0])
                v.tensor_copy(v_p[:, :, 0:64], v_t[:, :, 0:64])
                v.tensor_copy(v_p[:, :, 129:193], v_t[:, :, 64:128]).then_inc(vv.h, 1)
                vv.n += 1
            vp_ready = vv.n
            prev_vp_ready[0] = vp_ready

            for nq4 in range(NQ):
                q0 = nq4 * 512
                pva = 4 + 2 * (nq4 % 2)
                qkd = {}
                pvd = {}
                pv15_prev = prev_pv15[0]
                sv_base = sv.n
                expd = {kc: sv_base + kc + 1 for kc in range(KC)}

                def emit_qk(t, kc, q0=q0, qkd=qkd):
                    bA = 2 * (kc % 2)
                    t.matmul(PS[:, bA, :], k_p[0:64, kc * 128:(kc + 1) * 128],
                             q_p[0:64, q0:q0 + 512], start=True, stop=True)
                    t.matmul(PS[:, bA + 1, :], k_p[64:128, kc * 128:(kc + 1) * 128],
                             q_p[64:128, q0:q0 + 512], start=True, stop=True,
                             tile_position=(64, 0)).then_inc(tv.h, 1)
                    tv.n += 1
                    qkd[kc] = tv.n

                @blk.tensor
                def _(t, p=p, nq4=nq4, bias_done=bias_done):
                    if nq4 == 0:
                        t.wait_ge(vv.h, bias_done)
                    else:
                        t.wait_ge(sv.h, prev_exp_tail[0])
                    emit_qk(t, 0)
                    emit_qk(t, 1)

                for kc in range(KC):
                    @blk.tensor
                    def _(t, kc=kc, p=p, nq4=nq4, pva=pva, vp_ready=vp_ready):
                        if kc == 0 and prev_norm_done[0]:
                            t.wait_ge(vv.h, prev_norm_done[0])
                        if kc == 0 and nq4 == 0:
                            t.wait_ge(vv.h, vp_ready)
                        t.wait_ge(sv.h, expd[kc])
                        sl = expab2[:, (kc % 4) // 2, 1024 * (kc % 2):1024 * (kc % 2) + 1024]
                        t.matmul(PS[:, pva, :], v_p[:, kc, 0:128], sl[:, 0:512],
                                 start=(kc == 0), stop=(kc == KC - 1))
                        t.matmul(PS[:, pva + 1, :], v_p[:, kc, 65:193],
                                 sl[:, 512:1024],
                                 start=(kc == 0), stop=(kc == KC - 1)
                                 ).then_inc(tv.h, 1)
                        tv.n += 1
                        pvd[kc] = tv.n
                        if kc + 2 < KC:
                            emit_qk(t, kc + 2)

                    if kc == 8 and deferred_bc[0] is not None:
                        deferred_bc[0]()
                        deferred_bc[0] = None

                @blk.scalar
                def _(s, p=p, nq4=nq4, pv15_prev=pv15_prev):
                    for kc in range(KC):
                        bA = 2 * (kc % 2)
                        s.wait_ge(tv.h, qkd[kc])
                        if kc >= 4:
                            s.wait_ge(tv.h, pvd[kc - 4])
                        elif kc == 0 and pv15_prev:
                            s.wait_ge(tv.h, pv15_prev)
                        sl = expab2[:, (kc % 4) // 2, 1024 * (kc % 2):1024 * (kc % 2) + 1024]
                        s.raw.activation(sl,
                                         PS[:, bA:bA + 2, :].rearrange("p a b -> p (a b)"),
                                         AF.Exp, bias=bias_m[:, 0:1]).then_inc(sv.h, 1)
                        sv.n += 1
                assert sv.n == sv_base + KC

                prev_exp_tail[0] = expd[KC - 1]
                prev_pv15[0] = pvd[KC - 1]

                @blk.vector
                def _(v, pv_end=pvd[KC - 1], pva=pva):
                    v.wait_ge(tv.h, pv_end)
                    v.tensor_copy(pvstage[0:65, 0, :], PS[0:65, pva, :])
                    v.tensor_copy(pvstage[:, 1, :], PS[:, pva + 1, :]).then_inc(vv.h, 1)
                    vv.n += 1
                stage_done = vv.n

                @blk.vector
                def _(v):
                    v.reciprocal(pvstage[64:65, 0, :], pvstage[64:65, 0, :])
                    v.reciprocal(pvstage[0:1, 1, :], pvstage[0:1, 1, :])
                    v.tensor_copy(recT[64:65, :], pvstage[64:65, 0, :])
                    v.tensor_copy(recT[0:1, :], pvstage[0:1, 1, :]).then_inc(vv.h, 1)
                    vv.n += 1
                rec_done = vv.n
                last_rec_done[0] = rec_done

                def emit_bcast_norm(p=p, q0=q0, rec_done=rec_done, pva=pva):
                    @blk.tensor
                    def _(t):
                        t.wait_ge(vv.h, rec_done)
                        t.wait_ge(vv.h, last_bias_done[0])
                        t.matmul(PS[:, pva, :], ones2[64:65, 0:128], recT[64:65, :],
                                 start=True, stop=True)
                        t.matmul(PS[:, pva + 1, :], ones2[0:1, 0:128], recT[0:1, :],
                                 start=True, stop=True).then_inc(tv.h, 1)
                        tv.n += 1
                    bcast_t = tv.n

                    @blk.vector
                    def _(v, bcast_t=bcast_t):
                        v.wait_ge(tv.h, bcast_t)
                        v.tensor_tensor(attn_n[0:64, p, q0:q0 + 512],
                                        pvstage[0:64, 0, :], PS[0:64, pva, :], ALU.mult)
                        v.tensor_tensor(attn_n[64:128, p, q0:q0 + 512],
                                        pvstage[64:128, 1, :], PS[64:128, pva + 1, :],
                                        ALU.mult).then_inc(vv.h, 1)
                        vv.n += 1
                    prev_norm_done[0] = vv.n
                deferred_bc[0] = emit_bcast_norm

        deferred_bc[0]()
        deferred_bc[0] = None
        attn_done = prev_norm_done[0]

        es2.close()
        esAI.close()

        es3 = ExitStack()
        S3 = lambda n, sh, dt: es3.enter_context(nc.sbuf_tensor(n, sh, dt))
        wo_sb = S3("wo_sb", [128, FC, D], BF16)
        wo_fe = S3("wo_fe", [128, 2, S], BF16)
        wo_tok = S3("wo_tok", [128, TC, D], BF16)
        ga_bc = S3("ga_bc", [128, D], F32)
        grs = S3("grs", [128, D], F32)

        @blk.gpsimd
        def _(g):
            g.wait_ge(vv.h, attn_done)
            for dc in range(FC):
                g.dma_start(wo_sb[:, dc, :], wo_h[dc * 128:(dc + 1) * 128, :]
                            ).then_inc(lg.h, 16)
            g.dma_start(ga_bc[:, :], scr[6:7, 0:D].partition_broadcast(128)
                        ).then_inc(lg.h, 16)
        lg.n += 16 * 9
        wo_ld = lg.n

        wo_mm_hist = []
        wo_ev_hist = []
        wo_dmat_hist = []
        for oc in range(FC):
            for nq in range(NQ):
                i = oc * NQ + nq

                @blk.tensor
                def _(t, oc=oc, nq=nq, i=i):
                    if i == 0:
                        t.wait_ge(lg.h, wo_ld)
                        t.wait_ge(vv.h, attn_done)
                    if i >= 2:
                        t.wait_ge(vv.h, wo_ev_hist[i - 2])
                    for dc in range(FC):
                        ins = t.matmul(PS[:, i % 2, :], wo_sb[:, dc, oc * 128:(oc + 1) * 128],
                                       attn_n[:, dc, nq * 512:(nq + 1) * 512],
                                       start=(dc == 0), stop=(dc == FC - 1))
                    ins.then_inc(tv.h, 1)
                    tv.n += 1
                wo_mm_hist.append(tv.n)

                @blk.vector
                def _(v, oc=oc, nq=nq, i=i):
                    v.wait_ge(tv.h, wo_mm_hist[i])
                    if oc >= 2 and nq == 0:
                        ds_, dv_ = wo_dmat_hist[oc - 2]
                        v.wait_ge(ds_.h, dv_)
                    v.tensor_copy(wo_fe[:, oc % 2, nq * 512:(nq + 1) * 512],
                                  PS[:, i % 2, :]).then_inc(vv.h, 1)
                    vv.n += 1
                wo_ev_hist.append(vv.n)

            dsem = dts if oc % 2 == 0 else dt1

            @blk.sync
            def _(sy, oc=oc, dsem=dsem):
                sy.wait_ge(vv.h, wo_ev_hist[oc * NQ + NQ - 1])
                sy.dma_start_transpose(wo_tok[:, :, oc * 128:(oc + 1) * 128],
                                       wo_fe[:, oc % 2, :]).then_inc(dsem.h, 16)
            dsem.n += 16
            wo_dmat_hist.append((dsem, dsem.n))
        wo_t_done = [(dts, dts.n), (dt1, dt1.n)]

        res1_hist = []
        for tc in range(TC):
            @blk.vector
            def _(v, tc=tc):
                if tc == 0:
                    for ds_, dv_ in wo_t_done:
                        v.wait_ge(ds_.h, dv_)
                v.tensor_tensor(grs[:, :], wo_tok[:, tc, :], ga_bc[:, :], ALU.mult)
                v.tensor_tensor(x_tok[:, tc, :], x_tok[:, tc, :], grs[:, :], ALU.add
                                ).then_inc(vv.h, 1)
                vv.n += 1
            res1_hist.append(vv.n)
        res1_done = vv.n

        es3.close()
        esL2.close()

        es4 = ExitStack()
        S4 = lambda n, sh, dt: es4.enter_context(nc.sbuf_tensor(n, sh, dt))
        cb1_bc = S4("cb1_bc", [128, D], F32)
        sbv_bc = S4("sbv_bc", [128, D], F32)
        gb_bc = S4("gb_bc", [128, D], BF16)
        gbias_bc = S4("gbias_bc", [128, D], BF16)
        h2 = S4("h2", [128, D], F32)
        h2B = S4("h2B", [128, D], F32)
        tq = S4("tq", [128, D], F32)
        tqB = S4("tqB", [128, D], F32)
        dq = S4("dq", [128, D], F32)
        dqB = S4("dqB", [128, D], F32)
        sg1 = S4("sg1", [128, D], BF16)
        sg1B = S4("sg1B", [128, D], BF16)
        sg2 = S4("sg2", [128, D], BF16)
        sg2B = S4("sg2B", [128, D], BF16)
        q8s = S4("q8s", [128, D], I8)
        q8sB = S4("q8sB", [128, D], I8)
        mcolB = S4("mcolB", [128, 2], F32)
        q8tok = S4("q8tok", [128, 2, D], BF16)
        q8fe = S4("q8fe", [128, FC, QH], BF16)
        ffn_fe = S4("ffn_fe", [128, 2, QH], BF16)
        ffn_tok = S4("ffn_tok", [128, 8, D], BF16)
        grs2 = S4("grs2", [128, D], F32)

        @blk.gpsimd
        def _(g):
            g.wait_ge(vv.h, res1_done)
            g.dma_start(cb1_bc[:, :], scr[4:5, 0:D].partition_broadcast(128)).then_inc(lg.h, 16)
            g.dma_start(sbv_bc[:, :], scr[5:6, 0:D].partition_broadcast(128)).then_inc(lg.h, 16)
            g.dma_start(gb_bc[:, :], scr[7:8, 0:D].partition_broadcast(128)).then_inc(lg.h, 16)
            g.dma_start(gbias_bc[:, :], scr[8:9, 0:D].partition_broadcast(128)).then_inc(lg.h, 16)
        lg.n += 64
        p6_bc = lg.n

        @blk.vector
        def _(v):
            v.wait_ge(vv.h, res1_done)
            for tc in range(TC):
                v.tensor_scalar(h2[:, :], x_tok[:, tc, :], 0.0, 0.0, ALU.add, ALU.add,
                                accum_out=stx[:, tc:tc + 1])
                v.scalar_tensor_tensor(h2[:, :], x_tok[:, tc, :], 1.0, x_tok[:, tc, :],
                                       ALU.mult, ALU.mult, accum_out=stxx[:, tc:tc + 1])
            v.drain()
            v.tensor_scalar(mu3[:, :], stx[:, :], 1.0 / D, None, ALU.mult)
            v.drain()
            v.tensor_tensor(asb[:, :], mu3[:, :], mu3[:, :], ALU.mult)
            v.drain()
            v.scalar_tensor_tensor(rasb[:, :], stxx[:, :], 1.0 / D, asb[:, :],
                                   ALU.mult, ALU.subtract)
            v.drain()
            v.engine_nop().then_inc(vv.h, 1)
            vv.n += 1
        var3_done = vv.n

        @blk.scalar
        def _(s):
            s.wait_ge(vv.h, var3_done)
            s.activation(asb[:, :], rasb[:, :], AF.Ln, bias=sc1[:, 0:1])
            s.activation(r3c[:, :], asb[:, :], AF.Exp, scale=-0.5).then_inc(sv.h, 1)
            sv.n += 1
        r3_done = sv.n

        q8_dmat_hist = []
        q8i_hist = []
        ep_hist = []
        store_hist = []
        h2p, tqp, dqp = [h2, h2B], [tq, tqB], [dq, dqB]
        sg1p, sg2p, q8sp, mcolp = [sg1, sg1B], [sg2, sg2B], [q8s, q8sB], [mcol, mcolB]

        def emit_q_head(tc):
            P_ = tc % 2

            @blk.vector
            def _(v, tc=tc, P_=P_):
                if tc == 0:
                    v.wait_ge(sv.h, r3_done)
                    v.wait_ge(lg.h, p6_bc)
                if len(q8i_hist) >= 2:
                    v.wait_ge(sv.h, q8i_hist[-2])
                v.tensor_scalar(h2p[P_][:, :], x_tok[:, tc, :], mu3[:, tc:tc + 1],
                                r3c[:, tc:tc + 1], ALU.subtract, ALU.mult)
                v.tensor_tensor(h2p[P_][:, :], h2p[P_][:, :], cb1_bc[:, :], ALU.mult)
                v.tensor_tensor(h2p[P_][:, :], h2p[P_][:, :], sbv_bc[:, :], ALU.add)
                v.reduce_max(mcolp[P_][:, 0:1], h2p[P_][:, :], AX.X,
                             apply_absolute_value=True)
                v.drain()
                v.tensor_scalar_max(mcolp[P_][:, 0:1], mcolp[P_][:, 0:1], 1e-5)
                v.drain()
                v.reciprocal(mcolp[P_][:, 1:2], mcolp[P_][:, 0:1])
                v.drain()
                v.tensor_scalar(asb[:, tc:tc + 1], mcolp[P_][:, 1:2], 127.0, None,
                                ALU.mult)
                v.tensor_scalar(rasb[:, tc:tc + 1], mcolp[P_][:, 0:1], wsb[:, 0:1],
                                1.0 / 127.0, ALU.mult, ALU.mult)
                v.drain()
                v.engine_nop().then_inc(vv.h, 1)
                vv.n += 1
            h2_done = vv.n

            @blk.scalar
            def _(s, tc=tc, P_=P_, h2_done=h2_done):
                s.wait_ge(vv.h, h2_done)
                s.activation(tqp[P_][:, :], h2p[P_][:, :], AF.Copy,
                             scale=asb[:, tc:tc + 1])
                s.activation(q8sp[P_][:, :], h2p[P_][:, :], AF.Copy,
                             scale=asb[:, tc:tc + 1]).then_inc(sv.h, 1)
                sv.n += 1
            q8i_hist.append(sv.n)

        def emit_q_tail(tc, tci, q8i_done):
            P_ = tc % 2

            @blk.vector
            def _(v, P_=P_, q8i_done=q8i_done):
                v.wait_ge(sv.h, q8i_done)
                v.tensor_tensor(dqp[P_][:, :], tqp[P_][:, :], q8sp[P_][:, :],
                                ALU.subtract).then_inc(vv.h, 1)
                vv.n += 1
            d_done = vv.n

            @blk.scalar
            def _(s, P_=P_, d_done=d_done):
                s.wait_ge(vv.h, d_done)
                s.activation(sg1p[P_][:, :], dqp[P_][:, :], AF.Sign, bias=schalf[:, 0:1])
                s.activation(sg2p[P_][:, :], dqp[P_][:, :], AF.Sign, bias=schalf[:, 1:2]
                             ).then_inc(sv.h, 1)
                sv.n += 1
            sgn_done = sv.n

            @blk.vector
            def _(v, tc=tc, P_=P_, sgn_done=sgn_done):
                v.wait_ge(sv.h, sgn_done)
                if len(q8_dmat_hist) >= 2:
                    ds_, dv_ = q8_dmat_hist[-2]
                    v.wait_ge(ds_.h, dv_)
                v.tensor_tensor(sg1p[P_][:, :], sg1p[P_][:, :], sg2p[P_][:, :], ALU.add)
                v.scalar_tensor_tensor(q8tok[:, tc % 2, :], sg1p[P_][:, :], 0.5,
                                       q8sp[P_][:, :], ALU.mult, ALU.add
                                       ).then_inc(vv.h, 1)
                vv.n += 1
            q8_done = vv.n

            dsem = dts if tc % 2 == 0 else dt1

            @blk.sync
            def _(sy, tc=tc, tci=tci, q8_done=q8_done, dsem=dsem):
                sy.wait_ge(vv.h, q8_done)
                sy.dma_start_transpose(q8fe[:, :, tci * 128:(tci + 1) * 128],
                                       q8tok[:, tc % 2, :]).then_inc(dsem.h, 16)
            dsem.n += 16
            q8_dmat_hist.append((dsem, dsem.n))

        for rg in range(2):
            for tci in range(8):
                tc = rg * 8 + tci
                emit_q_head(tc)
                if tci >= 1:
                    emit_q_tail(tc - 1, tci - 1, q8i_hist[-2])
            emit_q_tail(rg * 8 + 7, 7, q8i_hist[-1])
            q8_all = [(dts, dts.n), (dt1, dt1.n)]

            f_mm_hist = []
            f_ev_hist = []
            f_dmat_hist = []
            for oc in range(FC):
                for n2 in range(2):
                    i = oc * 2 + n2

                    @blk.tensor
                    def _(t, rg=rg, oc=oc, n2=n2, i=i, q8_all=q8_all):
                        if i == 0:
                            for ds_, dv_ in q8_all:
                                t.wait_ge(ds_.h, dv_)
                        if i >= 2:
                            t.wait_ge(vv.h, f_ev_hist[i - 2])
                        for fc in range(FC):
                            ins = t.matmul(PS[:, i % 2, :], wt_ffn[:, fc, oc * 128:(oc + 1) * 128],
                                           q8fe[:, fc, n2 * 512:(n2 + 1) * 512],
                                           start=(fc == 0), stop=(fc == FC - 1))
                        ins.then_inc(tv.h, 1)
                        tv.n += 1
                    f_mm_hist.append(tv.n)

                    @blk.vector
                    def _(v, oc=oc, n2=n2, i=i):
                        v.wait_ge(tv.h, f_mm_hist[i])
                        if oc >= 2 and n2 == 0:
                            ds_, dv_ = f_dmat_hist[oc - 2]
                            v.wait_ge(ds_.h, dv_)
                        v.tensor_copy(ffn_fe[:, oc % 2, n2 * 512:(n2 + 1) * 512],
                                      PS[:, i % 2, :]).then_inc(vv.h, 1)
                        vv.n += 1
                    f_ev_hist.append(vv.n)

                dsem = dts if oc % 2 == 0 else dt1

                @blk.sync
                def _(sy, oc=oc, dsem=dsem):
                    sy.wait_ge(vv.h, f_ev_hist[oc * 2 + 1])
                    sy.dma_start_transpose(ffn_tok[:, :, oc * 128:(oc + 1) * 128],
                                           ffn_fe[:, oc % 2, :]).then_inc(dsem.h, 16)
                dsem.n += 16
                f_dmat_hist.append((dsem, dsem.n))
            f_t_done = [(dts, dts.n), (dt1, dt1.n)]

            for tci in range(8):
                tc = rg * 8 + tci

                @blk.vector
                def _(v, tc=tc, tci=tci, f_t_done=f_t_done):
                    if tci == 0:
                        for ds_, dv_ in f_t_done:
                            v.wait_ge(ds_.h, dv_)
                    v.tensor_scalar(sg1[:, :], ffn_tok[:, tci, :], rasb[:, tc:tc + 1],
                                    None, ALU.mult)
                    v.tensor_tensor(sg2[:, :], sg1[:, :], gb_bc[:, :], ALU.mult)
                    v.tensor_tensor(grs2[:, :], sg2[:, :], gbias_bc[:, :], ALU.add)
                    v.tensor_tensor(x_tok[:, tc, :], x_tok[:, tc, :], grs2[:, :], ALU.add
                                    ).then_inc(vv.h, 1)
                    vv.n += 1
                ep_hist.append(vv.n)

                @blk.sync
                def _(sy, tc=tc):
                    sy.wait_ge(vv.h, ep_hist[-1])
                    sy.dma_start(out_h[tc * 128:(tc + 1) * 128, :], x_tok[:, tc, :]
                                 ).then_inc(st.h, 16)
                    st.n += 16
                store_hist.append(st.n)

        @blk.sync
        def _(sy):
            sy.wait_ge(st.h, st.n)

        es4.close()
    return nc


_NC_CACHE = {}


def kernel(L0_attn_latents, L0_attn_ln_gamma, L0_attn_ln_beta, L0_attn_Wq,
           L0_attn_Wk, L0_attn_Wv, L0_attn_Wo, L0_ffn_packed_weight,
           L0_ffn_weight_scale, L0_ffn_bias, L0_c, L0_wmod):
    if "nc" not in _NC_CACHE:
        _NC_CACHE["nc"] = build_nc()
    nc = _NC_CACHE["nc"]
    x = np.ascontiguousarray(np.asarray(L0_attn_latents, dtype=np.float32))
    shared = {
        "ln_g": np.asarray(L0_attn_ln_gamma, np.float32).reshape(1, D),
        "ln_b": np.asarray(L0_attn_ln_beta, np.float32).reshape(1, D),
        "wq": np.ascontiguousarray(np.asarray(L0_attn_Wq, np.float32)),
        "wk": np.ascontiguousarray(np.asarray(L0_attn_Wk, np.float32)),
        "wv": np.ascontiguousarray(np.asarray(L0_attn_Wv, np.float32)),
        "wo": np.ascontiguousarray(np.asarray(L0_attn_Wo, np.float32)),
        "pk": np.ascontiguousarray(np.asarray(L0_ffn_packed_weight, np.int32
                                              ).reshape(D // 4, D)),
        "wsc": np.asarray(L0_ffn_weight_scale, np.float32).reshape(1, 1),
        "fbias": np.asarray(L0_ffn_bias, np.float32).reshape(1, D),
        "wmod": np.ascontiguousarray(np.asarray(L0_wmod, np.float32)),
    }
    in_maps = []
    for b in range(8):
        m = dict(shared)
        m["x"] = np.ascontiguousarray(x[b])
        m["cvec"] = np.ascontiguousarray(np.asarray(L0_c, np.float32)[b].reshape(1, D))
        in_maps.append(m)
    res = run_bass_kernel_spmd(nc, in_maps, core_ids=list(range(8)))
    return np.stack([res.results[b]["out"] for b in range(8)], axis=0)


# version 24 (final)
# speedup vs baseline: 1.0114x; 1.0037x over previous
import numpy as np
import ml_dtypes
from contextlib import ExitStack
import concourse.bass as bass
import concourse.mybir as mybir
from concourse.bass_utils import run_bass_kernel_spmd

F32 = mybir.dt.float32
BF16 = mybir.dt.bfloat16
I32 = mybir.dt.int32
I8 = mybir.dt.int8
AF = mybir.ActivationFunctionType
ALU = mybir.AluOpType
AX = mybir.AxisListType

D = 1024
S = 2048
TC = S // 128
FC = D // 128
NP = 8
QH = 1024
NQ = 4
KC = 16
EPS = 1e-5
QKS = float(1.0 / np.sqrt(np.sqrt(64.0)))


class Sem:
    def __init__(self, es, nc, name):
        self.h = es.enter_context(nc.semaphore(name))
        self.n = 0

    def inc(self, instr):
        self.n += 1
        return instr

    def dinc(self, instr):
        self.n += 16
        return instr


def _ap_tname(a):
    t = getattr(a, "tensor", None)
    return getattr(t, "name", None) if t is not None else None


class EngP:

    def __init__(self, eng, acc):
        self.__dict__["_e"] = eng
        self.__dict__["_acc"] = acc

    @property
    def raw(self):
        return self._e

    def drain(self):
        self._e.drain()
        self._acc.clear()

    def wait_ge(self, *a, **k):
        return self._e.wait_ge(*a, **k)

    def __getattr__(self, name):
        f = getattr(self._e, name)
        if not callable(f):
            return f
        acc = self._acc
        eng = self._e

        def wrap(*args, **kwargs):
            names = set()
            for a in list(args) + list(kwargs.values()):
                n = _ap_tname(a)
                if n is not None:
                    names.add(n)
            if names & acc:
                eng.drain()
                acc.clear()
            acc.update(names)
            return f(*args, **kwargs)
        return wrap


class BlkWrap:
    def __init__(self, blk, dirty):
        self._blk = blk
        self._dirty = dirty

    @property
    def vector(self):
        def deco(body):
            @self._blk.vector
            def _(v):
                body(EngP(v, self._dirty["v"]))
            return _
        return deco

    @property
    def scalar(self):
        def deco(body):
            @self._blk.scalar
            def _(s):
                body(EngP(s, self._dirty["s"]))
            return _
        return deco

    def __getattr__(self, name):
        return getattr(self._blk, name)


def build_nc():
    nc = bass.Bass("TRN2", target_bir_lowering=False)

    x_h = nc.dram_tensor("x", [S, D], F32, kind="ExternalInput").ap()
    g_h = nc.dram_tensor("ln_g", [1, D], F32, kind="ExternalInput").ap()
    be_h = nc.dram_tensor("ln_b", [1, D], F32, kind="ExternalInput").ap()
    wq_h = nc.dram_tensor("wq", [D, D], F32, kind="ExternalInput").ap()
    wk_h = nc.dram_tensor("wk", [D, D], F32, kind="ExternalInput").ap()
    wv_h = nc.dram_tensor("wv", [D, D], F32, kind="ExternalInput").ap()
    wo_h = nc.dram_tensor("wo", [D, D], F32, kind="ExternalInput").ap()
    pk_h = nc.dram_tensor("pk", [D // 4, D], I32, kind="ExternalInput").ap()
    ws_h = nc.dram_tensor("wsc", [1, 1], F32, kind="ExternalInput").ap()
    fb_h = nc.dram_tensor("fbias", [1, D], F32, kind="ExternalInput").ap()
    c_h = nc.dram_tensor("cvec", [1, D], F32, kind="ExternalInput").ap()
    wm_h = nc.dram_tensor("wmod", [D, 6 * D], F32, kind="ExternalInput").ap()
    out_h = nc.dram_tensor("out", [S, D], F32, kind="ExternalOutput").ap()
    scr = nc.dram_tensor("scr_rows", [12, S], F32, kind="Internal").ap()

    es = ExitStack()
    SB = lambda n, sh, dt: es.enter_context(nc.sbuf_tensor(n, sh, dt))

    x_tok = SB("x_tok", [128, TC, D], F32)
    PS = es.enter_context(nc.psum_tensor("PS", [128, 8, 512], F32))
    sc1 = SB("sc1", [128, 1], F32)
    schalf = SB("schalf", [128, 2], F32)
    identf = SB("identf", [1, 1], F32)
    identb = SB("identb", [1, 1], BF16)
    ident128 = SB("ident128", [128, 128], BF16)
    ij128 = SB("ij128", [128, 128], I32)
    onesr = SB("onesr", [1, 128], BF16)
    scT = SB("scT", [128, 8], BF16)
    hsel = SB("hsel", [128, 2], BF16)
    bias_m = SB("bias_m", [128, 2], F32)
    nq2t = SB("nq2t", [2, 2], F32)
    negm = SB("negm", [2, 1], F32)
    negmT = SB("negmT", [1, 2], BF16)
    identf2 = SB("identf2", [2, 2], F32)
    wsb = SB("wsb", [128, 1], F32)
    scal = SB("scal", [1, 16], F32)
    eps1 = SB("eps1", [1, 1], F32)
    mu3 = SB("mu3", [128, TC], F32)
    r3c = SB("r3c", [128, TC], F32)
    asb = SB("asb", [128, TC], F32)
    rasb = SB("rasb", [128, TC], F32)
    stx = SB("stx", [128, TC], F32)
    stxx = SB("stxx", [128, TC], F32)
    mcol = SB("mcol", [128, 2], F32)
    wt_ffn = SB("wt_ffn", [128, FC, D], BF16)

    sems = {}
    for s_ in ("ld", "lg", "dt", "tv", "vv", "sv", "st", "wr", "ig",
               "bc", "pk", "d1"):
        sems[s_] = Sem(es, nc, "sm_" + s_)
    ld, lg, dts = sems["ld"], sems["lg"], sems["dt"]
    tv, vv, sv = sems["tv"], sems["vv"], sems["sv"]
    st, wr, ig = sems["st"], sems["wr"], sems["ig"]
    bc, pk, dt1 = sems["bc"], sems["pk"], sems["d1"]

    blk = BlkWrap(es.enter_context(nc.Block()), {"v": set(), "s": set()})

    def r3d(ap, w=512):
        return ap.rearrange("p (a b) -> p a b", b=w)

    with es:
        @blk.sync
        def _(sy):
            for tc in range(TC):
                sy.dma_start(x_tok[:, tc, :], x_h[tc * 128:(tc + 1) * 128, :]
                             ).then_inc(ld.h, 16)
                ld.n += 16
        x_loaded = ld.n

        esAI = ExitStack()
        ai = esAI.enter_context(nc.sbuf_tensor("ai", [128, FC, S], BF16, side="right"))
        es1 = ExitStack()
        S1 = lambda n, sh, dt: es1.enter_context(nc.sbuf_tensor(n, sh, dt))
        pool = S1("pool", [1, 10, 512], F32)
        uab = S1("uab", [1, S], BF16)
        agb = S1("agb", [1, D], BF16)
        Blhs = S1("Blhs", [4, D], BF16)
        Brhs = S1("Brhs", [4, S], BF16)
        statx = S1("statx", [128, FC, 4], BF16)
        statxx = S1("statxx", [128, FC, 2], BF16)
        es1a = ExitStack()
        S1A = lambda n, sh, dt: es1a.enter_context(nc.sbuf_tensor(n, sh, dt))
        vrow = S1A("vrow", [1, 12, D], BF16)
        frow = S1A("frow", [1, 4, D], F32)
        rowf = S1A("rowf", [1, D], F32)
        wmbuf = S1A("wmbuf", [128, 2, 3072], BF16)

        @blk.gpsimd
        def _(g):
            g.dma_start(wsb[:, 0:1], ws_h.partition_broadcast(128)).then_inc(lg.h, 16)
            g.dma_start(frow[0:1, 0, :], c_h).then_inc(lg.h, 16)
            g.dma_start(frow[0:1, 2, :], g_h).then_inc(lg.h, 16)
            g.dma_start(frow[0:1, 3, :], be_h).then_inc(lg.h, 16)
            g.dma_start(vrow[0:1, 11, :], fb_h).then_inc(lg.h, 16)
            g.iota(ij128[:], [[1, 128]], base=0, channel_multiplier=-1).then_inc(ig.h, 1)
        lg.n += 80
        ig.n += 1
        base_ld = lg.n

        @blk.vector
        def _(v):
            v.memset(sc1[:], EPS)
            v.memset(eps1[:], EPS)
            v.memset(schalf[:, 0:1], -0.5)
            v.memset(schalf[:, 1:2], 0.5)
            v.memset(identf[:], 1.0)
            v.memset(identb[:], 1.0)
            v.memset(onesr[:], 1.0)
            v.memset(hsel[:, 0:1], 1.0)
            v.memset(hsel[:, 1:2], 0.0)
            v.memset(statx[:, :, 0:1], 1.0)
            v.memset(statxx[:, :, 0:1], 1.0)
            v.wait_ge(lg.h, base_ld)
            v.wait_ge(ig.h, 1)
            v.tensor_scalar(ident128[:], ij128[:], 0, None, ALU.is_equal)
            v.tensor_scalar(identf2[:], ij128[0:2, 0:2], 0, None, ALU.is_equal
                            ).then_inc(vv.h, 1)
            vv.n += 1
        consts_done = vv.n

        @blk.scalar
        def _(s):
            s.wait_ge(lg.h, base_ld)
            s.activation(frow[0:1, 1, :], frow[0:1, 0, :], AF.Silu
                         ).then_inc(sv.h, 1)
            sv.n += 1
        silu_done = sv.n

        @blk.tensor
        def _(t):
            t.wait_ge(sv.h, silu_done)
            t.wait_ge(vv.h, consts_done)
            for k in range(8):
                ins = t.transpose(PS[:, 6, k:k + 1], frow[0:1, 1, k * 128:(k + 1) * 128],
                                  identf[0:1, 0:1])
            ins.then_inc(tv.h, 1)
            tv.n += 1
        scT_tp = tv.n

        @blk.vector
        def _(v):
            v.wait_ge(tv.h, scT_tp)
            v.tensor_copy(scT[:, 0:8], PS[:, 6, 0:8]).then_inc(vv.h, 1)
            vv.n += 1
        scT_done = vv.n

        mod_mm_hist = []
        mod_evac = [scT_done]
        chunk_ld = []
        for hf in range(2):
            for kc in range(FC):
                i = hf * FC + kc

                psem = lg if i % 2 == 0 else bc

                @blk.gpsimd
                def _(g, hf=hf, kc=kc, i=i, psem=psem):
                    if i == 0:
                        g.wait_ge(lg.h, base_ld)
                    if i >= 2:
                        g.wait_ge(tv.h, mod_mm_hist[i - 2])
                    g.dma_start(wmbuf[:, i % 2, :],
                                wm_h[kc * 128:(kc + 1) * 128,
                                     hf * 3072:(hf + 1) * 3072]
                                ).then_inc(psem.h, 16)
                psem.n += 16
                chunk_ld.append((psem, psem.n))

                @blk.tensor
                def _(t, hf=hf, kc=kc, i=i):
                    sem_i, cnt_i = chunk_ld[i]
                    t.wait_ge(sem_i.h, cnt_i)
                    if kc == 0:
                        t.wait_ge(vv.h, mod_evac[-1])
                    for ncol in range(6):
                        ins = t.matmul(PS[0:1, ncol, :], scT[:, kc:kc + 1],
                                       wmbuf[:, kc % 2, ncol * 512:(ncol + 1) * 512],
                                       start=(kc == 0), stop=(kc == FC - 1))
                    ins.then_inc(tv.h, 1)
                    tv.n += 1
                mod_mm_hist.append(tv.n)

            @blk.vector
            def _(v, hf=hf):
                v.wait_ge(tv.h, mod_mm_hist[-1])
                if hf == 0:
                    for j in range(3):
                        ins = v.tensor_copy(r3d(vrow[0:1, j, :]), PS[0:1, 2 * j:2 * j + 2, :])
                else:
                    v.tensor_copy(r3d(rowf[0:1, :]), PS[0:1, 0:2, :])
                    ins = v.tensor_copy(r3d(vrow[0:1, 5, :]), PS[0:1, 4:6, :])
                ins.then_inc(vv.h, 1)
                vv.n += 1
            mod_evac.append(vv.n)
        mod_h1_evac = mod_evac[-1]

        @blk.sync
        def _(sy):
            sy.wait_ge(vv.h, mod_h1_evac)
            sy.dma_start(scr[5:6, 0:D], rowf[0:1, :]).then_inc(wr.h, 16)
            wr.n += 16
        sb_written = wr.n

        @blk.vector
        def _(v):
            v.wait_ge(wr.h, sb_written)
            v.tensor_scalar(r3d(rowf[0:1, :]), PS[0:1, 2:4, :], 1.0, None, ALU.add
                            ).then_inc(vv.h, 1)
            vv.n += 1
        cb1_row = vv.n

        @blk.sync
        def _(sy):
            sy.wait_ge(vv.h, cb1_row)
            sy.dma_start(scr[4:5, 0:D], rowf[0:1, :]).then_inc(wr.h, 16)
            wr.n += 16
        cb1_written = wr.n

        @blk.vector
        def _(v):
            v.wait_ge(wr.h, cb1_written)
            v.tensor_scalar(vrow[0:1, 6, :], vrow[0:1, 1, :], 1.0, None, ALU.add)
            for (src, dst) in ((2, 7), (5, 8)):
                v.tensor_scalar(vrow[0:1, 9, :], vrow[0:1, src, :], 0.0, None, ALU.is_ge)
                v.tensor_scalar(vrow[0:1, 9, :], vrow[0:1, 9, :], 2.0, -1.0, ALU.mult, ALU.add)
                v.tensor_scalar(vrow[0:1, 10, :], vrow[0:1, src, :], -1.0, None, ALU.mult)
                v.tensor_tensor(vrow[0:1, 10, :], vrow[0:1, src, :], vrow[0:1, 10, :], ALU.max)
                v.tensor_scalar_max(vrow[0:1, 10, :], vrow[0:1, 10, :], 0.25)
                v.tensor_tensor(vrow[0:1, dst, :], vrow[0:1, 9, :], vrow[0:1, 10, :], ALU.mult)
            v.reduce_sum(scal[0:1, 0:1], vrow[0:1, 6, :], AX.X)
            v.tensor_tensor(vrow[0:1, 9, :], vrow[0:1, 6, :], vrow[0:1, 6, :], ALU.mult)
            v.reduce_sum(scal[0:1, 1:2], vrow[0:1, 9, :], AX.X)
            v.tensor_tensor(vrow[0:1, 10, :], vrow[0:1, 6, :], vrow[0:1, 0, :], ALU.mult)
            v.reduce_sum(scal[0:1, 2:3], vrow[0:1, 10, :], AX.X)
            v.tensor_tensor(vrow[0:1, 4, :], vrow[0:1, 0, :], vrow[0:1, 0, :], ALU.mult)
            v.reduce_sum(scal[0:1, 3:4], vrow[0:1, 4, :], AX.X)
            v.reduce_sum(scal[0:1, 4:5], vrow[0:1, 0, :], AX.X)
            v.drain()
            v.tensor_scalar(scal[0:1, 3:4], scal[0:1, 3:4], 1.0 / D, None, ALU.mult)
            v.tensor_scalar(scal[0:1, 4:5], scal[0:1, 4:5], 1.0 / D, None, ALU.mult
                            ).then_inc(vv.h, 1)
            vv.n += 1
        drows_done = vv.n

        @blk.tensor
        def _(t):
            t.wait_ge(vv.h, drows_done)
            psb = PS[:, 7, :].bitcast(BF16)
            for (j, r) in enumerate((6, 9, 10)):
                for k in range(8):
                    c = 2 * (j * 8 + k)
                    ins = t.transpose(psb[:, c:c + 1],
                                      vrow[0:1, r, k * 128:(k + 1) * 128], identb[0:1, 0:1])
            ins.then_inc(tv.h, 1)
            tv.n += 1
        stat_tp = tv.n

        @blk.vector
        def _(v):
            v.wait_ge(tv.h, stat_tp)
            psb = PS[:, 7, :].bitcast(BF16)
            v.tensor_copy(statx[:, :, 1:2], psb[:, 0:16:2].unsqueeze(2))
            v.tensor_copy(statx[:, :, 2:3], psb[:, 16:32:2].unsqueeze(2))
            v.tensor_copy(statx[:, :, 3:4], psb[:, 32:48:2].unsqueeze(2))
            v.tensor_copy(statxx[:, :, 1:2], psb[:, 16:32:2].unsqueeze(2)).then_inc(vv.h, 1)
            vv.n += 1
        stat_lhs_done = vv.n

        def row_to_scr(make_row, scr_row, width=D):
            @blk.vector
            def _(v):
                make_row(v)
                v.engine_nop().then_inc(vv.h, 1)
                vv.n += 1
            r_t = vv.n

            @blk.sync
            def _(sy, r_t=r_t, scr_row=scr_row, width=width):
                sy.wait_ge(vv.h, r_t)
                sy.dma_start(scr[scr_row:scr_row + 1, 0:width], rowf[0:1, 0:width]
                             ).then_inc(wr.h, 16)
                wr.n += 16
            return wr.n

        def _mk(slot):
            def f(v):
                v.wait_ge(wr.h, last_w[0])
                v.tensor_copy(rowf[0:1, :], vrow[0:1, slot, :])
            return f
        last_w = [cb1_written]
        last_w[0] = row_to_scr(_mk(7), 6)
        last_w[0] = row_to_scr(_mk(8), 7)

        def _mk_gbias(v):
            v.wait_ge(wr.h, last_w[0])
            v.tensor_tensor(rowf[0:1, :], vrow[0:1, 8, :], vrow[0:1, 11, :], ALU.mult)
        last_w[0] = row_to_scr(_mk_gbias, 8)

        def _mk_sg(v):
            v.wait_ge(wr.h, last_w[0])
            v.tensor_tensor(rowf[0:1, :], vrow[0:1, 0, :], frow[0:1, 2, :], ALU.mult)
        last_w[0] = row_to_scr(_mk_sg, 1)

        def _mk_ag(v):
            v.wait_ge(wr.h, last_w[0])
            v.tensor_tensor(rowf[0:1, :], vrow[0:1, 6, :], frow[0:1, 2, :], ALU.mult)
            v.tensor_copy(agb[0:1, :], rowf[0:1, :])
        last_w[0] = row_to_scr(_mk_ag, 2)

        @blk.sync
        def _(sy):
            sy.dma_start(scr[3:4, 0:D], frow[0:1, 2, :]).then_inc(wr.h, 16)
            sy.dma_start(scr[0:1, 0:D], frow[0:1, 3, :]).then_inc(wr.h, 16)
            wr.n += 32
        blhs_written = wr.n

        @blk.gpsimd
        def _(g):
            g.wait_ge(wr.h, blhs_written)
            g.dma_start(Blhs[0:4, 0:D], scr[0:4, 0:D]).then_inc(lg.h, 16)
        lg.n += 16
        blhs_ld = lg.n

        es1a.close()
        es1b = ExitStack()
        S1B = lambda n, sh, dt: es1b.enter_context(nc.sbuf_tensor(n, sh, dt))
        xfe = S1B("xfe", [128, FC, S], BF16)
        xbf = S1B("xbf", [128, 2, D], BF16)
        x2c = S1B("x2c", [128, 2, 512], BF16)
        ais = S1B("ais", [128, S], F32)
        dmat_hist = []
        for tc in range(TC):
            dsem_x = dts if tc % 2 == 0 else dt1

            @blk.vector
            def _(v, tc=tc):
                if tc == 0:
                    v.wait_ge(ld.h, x_loaded)
                    v.wait_ge(tv.h, stat_tp)
                    v.wait_ge(wr.h, blhs_written)
                    v.wait_ge(lg.h, blhs_ld)
                elif tc >= 2:
                    ds_, dv_ = dmat_hist[tc - 2]
                    v.wait_ge(ds_.h, dv_)
                v.tensor_copy(xbf[:, tc % 2, :], x_tok[:, tc, :]).then_inc(vv.h, 1)
                vv.n += 1
            xbf_c = vv.n

            @blk.sync
            def _(sy, tc=tc, xbf_c=xbf_c, dsem_x=dsem_x):
                sy.wait_ge(vv.h, xbf_c)
                sy.dma_start_transpose(xfe[:, :, tc * 128:(tc + 1) * 128],
                                       xbf[:, tc % 2, :]).then_inc(dsem_x.h, 16)
            dsem_x.n += 16
            dmat_hist.append((dsem_x, dsem_x.n))
        xfe_done = [(dts, dts.n), (dt1, dt1.n)]

        stat_ev_hist = []
        srow_hist = []
        P = lambda i: pool[0:1, i, :]
        SA, SAA, SAS = scal[0:1, 0:1], scal[0:1, 1:2], scal[0:1, 2:3]
        SSS_D, SS_D = scal[0:1, 3:4], scal[0:1, 4:5]
        for nq in range(NQ):
            q0 = nq * 512
            sq_hist = []
            mm_hist = []
            for fc in range(FC):
                @blk.vector
                def _(v, fc=fc, nq=nq, q0=q0):
                    if fc == 0:
                        for ds_, dv_ in (dmat_hist[nq * 4 + 2], dmat_hist[nq * 4 + 3]):
                            v.wait_ge(ds_.h, dv_)
                    if nq == 0 and fc == 0:
                        v.wait_ge(vv.h, stat_lhs_done)
                    if fc >= 2:
                        v.wait_ge(tv.h, mm_hist[fc - 2])
                    v.tensor_tensor(x2c[:, fc % 2, :], xfe[:, fc, q0:q0 + 512],
                                    xfe[:, fc, q0:q0 + 512], ALU.mult).then_inc(vv.h, 1)
                    vv.n += 1
                sq_hist.append(vv.n)

                @blk.tensor
                def _(t, fc=fc, nq=nq, q0=q0):
                    t.wait_ge(vv.h, sq_hist[fc])
                    if fc == 0 and nq > 0:
                        t.wait_ge(vv.h, stat_ev_hist[nq - 1])
                    for s_i in range(4):
                        t.matmul(PS[0:1, s_i, :], statx[:, fc, s_i:s_i + 1],
                                 xfe[:, fc, q0:q0 + 512],
                                 start=(fc == 0), stop=(fc == FC - 1))
                    for s_i in range(2):
                        ins = t.matmul(PS[0:1, 4 + s_i, :], statxx[:, fc, s_i:s_i + 1],
                                       x2c[:, fc % 2, :],
                                       start=(fc == 0), stop=(fc == FC - 1))
                    ins.then_inc(tv.h, 1)
                    tv.n += 1
                mm_hist.append(tv.n)
            stats_done = tv.n

            @blk.vector
            def _(v, stats_done=stats_done, nq=nq):
                v.wait_ge(tv.h, stats_done)
                if nq >= 1:
                    v.wait_ge(wr.h, srow_hist[nq - 1])
                for s_i in range(6):
                    ins = v.tensor_copy(P(s_i), PS[0:1, s_i, :])
                ins.then_inc(vv.h, 1)
                vv.n += 1
            stat_ev_hist.append(vv.n)

            @blk.vector
            def _(v):
                v.tensor_scalar(P(0), P(0), 1.0 / D, None, ALU.mult)
                v.tensor_tensor(P(6), P(0), P(0), ALU.mult)
                v.scalar_tensor_tensor(P(7), P(4), 1.0 / D, P(6), ALU.mult, ALU.subtract)
                v.engine_nop().then_inc(vv.h, 1)
                vv.n += 1
            alg1 = vv.n

            @blk.scalar
            def _(s, alg1=alg1):
                s.wait_ge(vv.h, alg1)
                s.activation(P(8), P(7), AF.Ln, bias=eps1[0:1, 0:1])
                s.activation(P(7), P(8), AF.Exp, scale=-0.5).then_inc(sv.h, 1)
                sv.n += 1
            alg_s1 = sv.n

            @blk.vector
            def _(v, alg_s1=alg_s1):
                v.wait_ge(sv.h, alg_s1)
                v.scalar_tensor_tensor(P(6), P(0), SA, P(1), ALU.mult, ALU.subtract)
                v.tensor_tensor(P(8), P(6), P(7), ALU.mult)
                v.tensor_scalar(P(8), P(8), -1.0 / D, SS_D, ALU.mult, ALU.add)
                v.scalar_tensor_tensor(P(6), P(0), SAA, P(2), ALU.mult, ALU.subtract)
                v.tensor_tensor(P(9), P(0), P(6), ALU.mult)
                v.tensor_tensor(P(6), P(0), P(2), ALU.mult)
                v.tensor_tensor(P(2), P(5), P(9), ALU.add)
                v.tensor_tensor(P(9), P(2), P(6), ALU.subtract)
                v.scalar_tensor_tensor(P(6), P(0), SAS, P(3), ALU.mult, ALU.subtract)
                v.tensor_tensor(P(2), P(7), P(6), ALU.mult)
                v.tensor_tensor(P(6), P(7), P(7), ALU.mult)
                v.tensor_tensor(P(3), P(6), P(9), ALU.mult)
                v.scalar_tensor_tensor(P(9), P(2), -2.0, P(3), ALU.mult, ALU.add)
                v.tensor_scalar(P(9), P(9), 1.0 / D, SSS_D, ALU.mult, ALU.add)
                v.tensor_tensor(P(6), P(8), P(8), ALU.mult)
                v.tensor_tensor(P(9), P(9), P(6), ALU.subtract)
                v.engine_nop().then_inc(vv.h, 1)
                vv.n += 1
            alg2 = vv.n

            @blk.scalar
            def _(s, alg2=alg2):
                s.wait_ge(vv.h, alg2)
                s.activation(P(6), P(9), AF.Ln, bias=eps1[0:1, 0:1])
                s.activation(P(9), P(6), AF.Exp, scale=-0.5).then_inc(sv.h, 1)
                sv.n += 1
            alg_s2 = sv.n

            @blk.vector
            def _(v, q0=q0, alg_s2=alg_s2):
                v.wait_ge(sv.h, alg_s2)
                v.tensor_tensor(P(6), P(7), P(9), ALU.mult)
                v.tensor_copy(uab[0:1, q0:q0 + 512], P(6))
                v.tensor_tensor(P(2), P(0), P(6), ALU.mult)
                v.tensor_scalar(P(2), P(2), -1.0, None, ALU.mult)
                v.tensor_tensor(P(3), P(8), P(9), ALU.mult)
                v.tensor_scalar(P(3), P(3), -1.0, None, ALU.mult)
                v.engine_nop().then_inc(vv.h, 1)
                vv.n += 1
            alg3 = vv.n

            @blk.sync
            def _(sy, q0=q0, alg3=alg3, nq=nq):
                sy.wait_ge(vv.h, alg3)
                if nq == 0:
                    sy.wait_ge(wr.h, blhs_written)
                sy.dma_start(scr[9:10, q0:q0 + 512], P(9)).then_inc(wr.h, 16)
                sy.dma_start(scr[10:11, q0:q0 + 512], P(2)).then_inc(wr.h, 16)
                sy.dma_start(scr[11:12, q0:q0 + 512], P(3)).then_inc(wr.h, 16)
                wr.n += 48
            srow_hist.append(wr.n)
        brhs_written = wr.n

        @blk.gpsimd
        def _(g):
            g.wait_ge(wr.h, brhs_written)
            g.dma_start(Brhs[1:4, 0:S], scr[9:12, 0:S]).then_inc(lg.h, 16)
        lg.n += 16
        brhs_ld = lg.n

        @blk.vector
        def _(v):
            v.wait_ge(lg.h, brhs_ld)
            v.memset(Brhs[0:1, :], 1.0).then_inc(vv.h, 1)
            vv.n += 1
        brhs_done = vv.n

        ab_hist = []
        ap_hist = []
        for fc in range(FC):
            @blk.tensor
            def _(t, fc=fc):
                if fc == 0:
                    t.wait_ge(vv.h, brhs_done)
                    t.wait_ge(lg.h, blhs_ld)
                else:
                    t.wait_ge(vv.h, ap_hist[fc - 1])
                for nq in range(NQ):
                    t.matmul(PS[:, nq, :], agb[0:1, fc * 128:(fc + 1) * 128],
                             uab[0:1, nq * 512:(nq + 1) * 512], start=True, stop=True)
                    ins = t.matmul(PS[:, 4 + nq, :], Blhs[0:4, fc * 128:(fc + 1) * 128],
                                   Brhs[0:4, nq * 512:(nq + 1) * 512], start=True, stop=True)
                ins.then_inc(tv.h, 1)
                tv.n += 1
            ab_hist.append(tv.n)

            @blk.vector
            def _(v, fc=fc):
                v.wait_ge(tv.h, ab_hist[fc])
                v.tensor_tensor(r3d(ais[:, :]), r3d(xfe[:, fc, :]), PS[:, 0:4, :], ALU.mult)
                v.tensor_tensor(r3d(ai[:, fc, :]), r3d(ais[:, :]), PS[:, 4:8, :], ALU.add
                                ).then_inc(vv.h, 1)
                vv.n += 1
            ap_hist.append(vv.n)
        ai_done = vv.n

        es1b.close()
        es1.close()

        esL2 = ExitStack()
        attn_n = esL2.enter_context(nc.sbuf_tensor("attn_n", [128, NP, S], BF16))
        es2 = ExitStack()
        S2 = lambda n, sh, dt: es2.enter_context(nc.sbuf_tensor(n, sh, dt))
        expab2 = S2("expab2", [128, 2, 2048], BF16)
        pvstage = S2("pvstage", [128, 2, 512], F32)
        recT = S2("recT", [128, 512], BF16)
        ones2 = S2("ones2", [128, 128], BF16)
        q_p = S2("q_p", [128, S], BF16)
        k_p = S2("k_p", [128, S], BF16)
        v_p = S2("v_p", [128, TC, 193], BF16)
        v_fe = S2("v_fe", [128, S], BF16)
        v_t = S2("v_t", [128, TC, 128], BF16)
        wst = S2("wst", [128, 3, FC, 128], BF16)
        pkb = S2("pkb", [128, 2, 1024], I32)
        encb = S2("encb", [128, D], I32)
        ternf = S2("ternf", [128, D], BF16)

        @blk.vector
        def _(v):
            v.drain()
            v.memset(v_p[:, :, 64:66], 1.0)
            v.memset(v_p[:, :, 66:129], 0.0)
            v.memset(ones2[0:1, :], 1.0)
            v.memset(ones2[64:65, :], 1.0).then_inc(vv.h, 1)
            vv.n += 1
        vp_init = vv.n

        pk_lds = []
        for rb in range(2):
            @blk.gpsimd
            def _(g, rb=rb):
                if rb == 0:
                    g.wait_ge(vv.h, ai_done)
                else:
                    g.wait_ge(pk.h, pk_lds[0])
                g.dma_start(pkb[:, rb, :], pk_h[rb * 128:(rb + 1) * 128, :]
                            ).then_inc(pk.h, 16)
            pk.n += 16
            pk_lds.append(pk.n)

        up_ev_hist = [ai_done]
        up_tp_hist = []
        for i in range(8):
            rb, s_i = i // 4, i % 4
            oc = 2 * s_i + rb

            @blk.vector
            def _(v, rb=rb, s_i=s_i, i=i):
                v.wait_ge(pk.h, pk_lds[rb])
                if i > 0:
                    v.wait_ge(tv.h, up_tp_hist[-1])
                v.tensor_scalar(encb[:, :], pkb[:, rb, :], 2 * s_i, 3,
                                ALU.logical_shift_right, ALU.bitwise_and)
                v.tensor_scalar(ternf[:, :], encb[:, :], 1.0, None, ALU.subtract
                                ).then_inc(vv.h, 1)
                vv.n += 1
            up_sq = vv.n

            @blk.tensor
            def _(t, i=i, up_sq=up_sq):
                t.wait_ge(vv.h, up_sq)
                if i == 0:
                    t.wait_ge(vv.h, ai_done)
                else:
                    t.wait_ge(vv.h, up_ev_hist[-1])
                psb = PS[:, 6:8, :].rearrange("p a b -> p (a b)").bitcast(BF16)
                for ic in range(FC):
                    ins = t.transpose(psb[:, ic * 128:(ic + 1) * 128],
                                      ternf[:, ic * 128:(ic + 1) * 128], ident128[:, :])
                ins.then_inc(tv.h, 1)
                tv.n += 1
            up_tp_hist.append(tv.n)
            up_tp = tv.n

            @blk.vector
            def _(v, oc=oc, up_tp=up_tp):
                v.wait_ge(tv.h, up_tp)
                psb = PS[:, 6:8, :].rearrange("p a b -> p (a b)").bitcast(BF16)
                v.tensor_copy(wt_ffn[:, :, oc * 128:(oc + 1) * 128],
                              psb[:, 0:1024].rearrange("p (a b) -> p a b", b=128)
                              ).then_inc(vv.h, 1)
                vv.n += 1
            up_ev_hist.append(vv.n)
        unpack_done = vv.n

        prev_proj_done = [0]
        prev_vp_ready = [0]
        prev_exp_tail = [0]
        prev_pv15 = [0]
        prev_norm_done = [0]
        last_rec_done = [0]
        last_bias_done = [0]
        deferred_bc = [None]
        for p in range(NP):
            @blk.gpsimd
            def _(g, p=p):
                if p == 0:
                    g.wait_ge(vv.h, ai_done)
                else:
                    g.wait_ge(tv.h, prev_proj_done[0])
                for (wi, wh) in ((0, wq_h), (1, wk_h), (2, wv_h)):
                    src = wh[:, p * 128:(p + 1) * 128]
                    for fc in range(FC):
                        g.dma_start(wst[:, wi, fc, :],
                                    src[fc * 128:(fc + 1) * 128, :]).then_inc(lg.h, 16)
            lg.n += 16 * 24
            wst_ld = lg.n

            qk_ev = []
            for (wi, dest) in ((0, q_p), (1, k_p)):
                @blk.tensor
                def _(t, p=p, wi=wi, wst_ld=wst_ld):
                    t.wait_ge(lg.h, wst_ld)
                    if p == 0 and wi == 0:
                        t.wait_ge(vv.h, unpack_done)
                        t.wait_ge(vv.h, ai_done)
                    if wi == 0 and p > 0:
                        t.wait_ge(sv.h, prev_exp_tail[0])
                    if wi == 1:
                        t.wait_ge(vv.h, qk_ev[0])
                    for nq in range(NQ):
                        for fc in range(FC):
                            ins = t.matmul(PS[:, nq, :], wst[:, wi, fc, :],
                                           ai[:, fc, nq * 512:(nq + 1) * 512],
                                           start=(fc == 0), stop=(fc == FC - 1))
                    ins.then_inc(tv.h, 1)
                    tv.n += 1
                mm_t = tv.n

                @blk.vector
                def _(v, dest=dest, mm_t=mm_t):
                    v.wait_ge(tv.h, mm_t)
                    v.tensor_scalar(r3d(dest[:, :]), PS[:, 0:4, :], QKS, None, ALU.mult
                                    ).then_inc(vv.h, 1)
                    vv.n += 1
                qk_ev.append(vv.n)
            k_ev = qk_ev[1]

            @blk.tensor
            def _(t, p=p):
                t.wait_ge(vv.h, qk_ev[1])
                for nq in range(NQ):
                    for fc in range(FC):
                        ins = t.matmul(PS[:, nq, :], wst[:, 2, fc, :],
                                       ai[:, fc, nq * 512:(nq + 1) * 512],
                                       start=(fc == 0), stop=(fc == FC - 1))
                ins.then_inc(tv.h, 1)
                tv.n += 1
            v_mm = tv.n

            @blk.vector
            def _(v, v_mm=v_mm):
                v.wait_ge(tv.h, v_mm)
                v.tensor_copy(r3d(v_fe[:, :]), PS[:, 0:4, :]).then_inc(vv.h, 1)
                vv.n += 1
            v_fe_done = vv.n

            @blk.sync
            def _(sy, v_fe_done=v_fe_done, p=p):
                sy.wait_ge(vv.h, v_fe_done)
                if p > 0:
                    sy.wait_ge(vv.h, prev_vp_ready[0])
                sy.dma_start_transpose(v_t[:, :, :], v_fe[:, :]).then_inc(dts.h, 16)
                dts.n += 16
            v_tp = dts.n

            proj_done = tv.n
            prev_proj_done[0] = proj_done

            nrm_mm = []
            sq_hist = []
            for (j, srcb) in ((0, q_p), (1, k_p)):
                for nq in range(NQ):
                    i2 = j * NQ + nq

                    @blk.vector
                    def _(v, j=j, srcb=srcb, nq=nq, i2=i2, k_ev=k_ev, p=p):
                        if i2 == 0:
                            v.wait_ge(vv.h, k_ev)
                            if p > 0:
                                v.wait_ge(tv.h, prev_pv15[0])
                        if i2 >= 3:
                            v.wait_ge(tv.h, nrm_mm[i2 - 3])
                        sl = expab2[:, (i2 % 4) // 2,
                                    512 * (i2 % 2):512 * (i2 % 2) + 512]
                        v.tensor_tensor(sl,
                                        srcb[:, nq * 512:(nq + 1) * 512],
                                        srcb[:, nq * 512:(nq + 1) * 512], ALU.mult
                                        ).then_inc(vv.h, 1)
                        vv.n += 1
                    sq_hist.append(vv.n)

                    @blk.tensor
                    def _(t, j=j, nq=nq, i2=i2):
                        t.wait_ge(vv.h, sq_hist[i2])
                        sl = expab2[:, (i2 % 4) // 2,
                                    512 * (i2 % 2):512 * (i2 % 2) + 512]
                        t.matmul(PS[0:1, nq, :], hsel[:, 0:1], sl,
                                 start=True, stop=True).then_inc(tv.h, 1)
                        tv.n += 1
                    nrm_mm.append(tv.n)

                @blk.vector
                def _(v, j=j):
                    v.wait_ge(tv.h, nrm_mm[j * NQ + NQ - 1])
                    v.reduce_max(nq2t[0:1, j:j + 1],
                                 PS[0:1, 0:4, :].rearrange("p a b -> p (a b)"), AX.X
                                 ).then_inc(vv.h, 1)
                    vv.n += 1
            @blk.vector
            def _(v):
                v.drain()
                v.tensor_tensor(negm[0:1, 0:1], nq2t[0:1, 0:1], nq2t[0:1, 1:2], ALU.mult
                                ).then_inc(vv.h, 1)
                vv.n += 1
            m2_d = vv.n

            @blk.scalar
            def _(s, m2_d=m2_d):
                s.wait_ge(vv.h, m2_d)
                s.activation(negm[0:1, 0:1], negm[0:1, 0:1], AF.Ln, bias=sc1[0:1, 0:1])
                s.activation(negm[0:1, 0:1], negm[0:1, 0:1], AF.Exp, scale=0.5
                             ).then_inc(sv.h, 1)
                sv.n += 1
            m_d = sv.n

            @blk.vector
            def _(v, m_d=m_d):
                v.wait_ge(sv.h, m_d)
                v.tensor_scalar(negmT[0:1, 0:1], negm[0:1, 0:1], -1.0, None, ALU.mult
                                ).then_inc(vv.h, 1)
                vv.n += 1
            negm_d = vv.n

            @blk.tensor
            def _(t, negm_d=negm_d, p=p):
                t.wait_ge(vv.h, negm_d)
                if p > 0:
                    t.wait_ge(vv.h, last_rec_done[0])
                t.matmul(PS[:, 7, 0:1], ones2[0:1, 0:128], negmT[0:1, 0:1],
                         start=True, stop=True).then_inc(tv.h, 1)
                tv.n += 1
            bb = tv.n

            @blk.vector
            def _(v, bb=bb):
                v.wait_ge(tv.h, bb)
                v.tensor_copy(bias_m[:, 0:1], PS[:, 7, 0:1]).then_inc(vv.h, 1)
                vv.n += 1
            bias_done = vv.n
            last_bias_done[0] = bias_done

            @blk.vector
            def _(v, v_tp=v_tp, p=p):
                v.wait_ge(dts.h, v_tp)
                if p == 0:
                    v.wait_ge(vv.h, vp_init)
                else:
                    v.wait_ge(tv.h, prev_pv15[0])
                v.tensor_copy(v_p[:, :, 0:64], v_t[:, :, 0:64])
                v.tensor_copy(v_p[:, :, 129:193], v_t[:, :, 64:128]).then_inc(vv.h, 1)
                vv.n += 1
            vp_ready = vv.n
            prev_vp_ready[0] = vp_ready

            for nq4 in range(NQ):
                q0 = nq4 * 512
                pva = 4 + 2 * (nq4 % 2)
                qkd = {}
                pvd = {}
                pv15_prev = prev_pv15[0]
                sv_base = sv.n
                expd = {kc: sv_base + kc + 1 for kc in range(KC)}

                def emit_qk(t, kc, q0=q0, qkd=qkd):
                    bA = 2 * (kc % 2)
                    t.matmul(PS[:, bA, :], k_p[0:64, kc * 128:(kc + 1) * 128],
                             q_p[0:64, q0:q0 + 512], start=True, stop=True)
                    t.matmul(PS[:, bA + 1, :], k_p[64:128, kc * 128:(kc + 1) * 128],
                             q_p[64:128, q0:q0 + 512], start=True, stop=True,
                             tile_position=(64, 0)).then_inc(tv.h, 1)
                    tv.n += 1
                    qkd[kc] = tv.n

                @blk.tensor
                def _(t, p=p, nq4=nq4, m2_d=m2_d):
                    if nq4 == 0:
                        t.wait_ge(vv.h, m2_d)
                    else:
                        t.wait_ge(sv.h, prev_exp_tail[0])
                    emit_qk(t, 0)
                    emit_qk(t, 1)

                for kc in range(KC):
                    @blk.tensor
                    def _(t, kc=kc, p=p, nq4=nq4, pva=pva, vp_ready=vp_ready):
                        if kc == 0 and prev_norm_done[0]:
                            t.wait_ge(vv.h, prev_norm_done[0])
                        if kc == 0 and nq4 == 0:
                            t.wait_ge(vv.h, vp_ready)
                        t.wait_ge(sv.h, expd[kc])
                        sl = expab2[:, (kc % 4) // 2, 1024 * (kc % 2):1024 * (kc % 2) + 1024]
                        t.matmul(PS[:, pva, :], v_p[:, kc, 0:128], sl[:, 0:512],
                                 start=(kc == 0), stop=(kc == KC - 1))
                        t.matmul(PS[:, pva + 1, :], v_p[:, kc, 65:193],
                                 sl[:, 512:1024],
                                 start=(kc == 0), stop=(kc == KC - 1)
                                 ).then_inc(tv.h, 1)
                        tv.n += 1
                        pvd[kc] = tv.n
                        if kc + 2 < KC:
                            emit_qk(t, kc + 2)

                    if kc == 8 and deferred_bc[0] is not None:
                        deferred_bc[0]()
                        deferred_bc[0] = None

                @blk.scalar
                def _(s, p=p, nq4=nq4, pv15_prev=pv15_prev, bias_done=bias_done):
                    for kc in range(KC):
                        bA = 2 * (kc % 2)
                        s.wait_ge(tv.h, qkd[kc])
                        if kc == 0 and nq4 == 0:
                            s.wait_ge(vv.h, bias_done)
                        if kc >= 4:
                            s.wait_ge(tv.h, pvd[kc - 4])
                        elif kc == 0 and pv15_prev:
                            s.wait_ge(tv.h, pv15_prev)
                        sl = expab2[:, (kc % 4) // 2, 1024 * (kc % 2):1024 * (kc % 2) + 1024]
                        s.raw.activation(sl,
                                         PS[:, bA:bA + 2, :].rearrange("p a b -> p (a b)"),
                                         AF.Exp, bias=bias_m[:, 0:1]).then_inc(sv.h, 1)
                        sv.n += 1
                assert sv.n == sv_base + KC

                prev_exp_tail[0] = expd[KC - 1]
                prev_pv15[0] = pvd[KC - 1]

                @blk.vector
                def _(v, pv_end=pvd[KC - 1], pva=pva):
                    v.wait_ge(tv.h, pv_end)
                    v.tensor_copy(pvstage[0:65, 0, :], PS[0:65, pva, :])
                    v.tensor_copy(pvstage[:, 1, :], PS[:, pva + 1, :]).then_inc(vv.h, 1)
                    vv.n += 1
                stage_done = vv.n

                @blk.vector
                def _(v):
                    v.reciprocal(pvstage[64:65, 0, :], pvstage[64:65, 0, :])
                    v.reciprocal(pvstage[0:1, 1, :], pvstage[0:1, 1, :])
                    v.tensor_copy(recT[64:65, :], pvstage[64:65, 0, :])
                    v.tensor_copy(recT[0:1, :], pvstage[0:1, 1, :]).then_inc(vv.h, 1)
                    vv.n += 1
                rec_done = vv.n
                last_rec_done[0] = rec_done

                def emit_bcast_norm(p=p, q0=q0, rec_done=rec_done, pva=pva):
                    @blk.tensor
                    def _(t):
                        t.wait_ge(vv.h, rec_done)
                        t.wait_ge(vv.h, last_bias_done[0])
                        t.matmul(PS[:, pva, :], ones2[64:65, 0:128], recT[64:65, :],
                                 start=True, stop=True)
                        t.matmul(PS[:, pva + 1, :], ones2[0:1, 0:128], recT[0:1, :],
                                 start=True, stop=True).then_inc(tv.h, 1)
                        tv.n += 1
                    bcast_t = tv.n

                    @blk.vector
                    def _(v, bcast_t=bcast_t):
                        v.wait_ge(tv.h, bcast_t)
                        v.tensor_tensor(attn_n[0:64, p, q0:q0 + 512],
                                        pvstage[0:64, 0, :], PS[0:64, pva, :], ALU.mult)
                        v.tensor_tensor(attn_n[64:128, p, q0:q0 + 512],
                                        pvstage[64:128, 1, :], PS[64:128, pva + 1, :],
                                        ALU.mult).then_inc(vv.h, 1)
                        vv.n += 1
                    prev_norm_done[0] = vv.n
                deferred_bc[0] = emit_bcast_norm

        deferred_bc[0]()
        deferred_bc[0] = None
        attn_done = prev_norm_done[0]

        es2.close()
        esAI.close()

        es3 = ExitStack()
        S3 = lambda n, sh, dt: es3.enter_context(nc.sbuf_tensor(n, sh, dt))
        wo_sb = S3("wo_sb", [128, FC, D], BF16)
        wo_fe = S3("wo_fe", [128, 2, S], BF16)
        wo_tok = S3("wo_tok", [128, TC, D], BF16)
        ga_bc = S3("ga_bc", [128, D], F32)
        grs = S3("grs", [128, D], F32)

        @blk.gpsimd
        def _(g):
            g.wait_ge(vv.h, attn_done)
            for dc in range(FC):
                g.dma_start(wo_sb[:, dc, :], wo_h[dc * 128:(dc + 1) * 128, :]
                            ).then_inc(lg.h, 16)
            g.dma_start(ga_bc[:, :], scr[6:7, 0:D].partition_broadcast(128)
                        ).then_inc(lg.h, 16)
        lg.n += 16 * 9
        wo_ld = lg.n

        wo_mm_hist = []
        wo_ev_hist = []
        wo_dmat_hist = []
        for oc in range(FC):
            for nq in range(NQ):
                i = oc * NQ + nq

                @blk.tensor
                def _(t, oc=oc, nq=nq, i=i):
                    if i == 0:
                        t.wait_ge(lg.h, wo_ld)
                        t.wait_ge(vv.h, attn_done)
                    if i >= 2:
                        t.wait_ge(vv.h, wo_ev_hist[i - 2])
                    for dc in range(FC):
                        ins = t.matmul(PS[:, i % 2, :], wo_sb[:, dc, oc * 128:(oc + 1) * 128],
                                       attn_n[:, dc, nq * 512:(nq + 1) * 512],
                                       start=(dc == 0), stop=(dc == FC - 1))
                    ins.then_inc(tv.h, 1)
                    tv.n += 1
                wo_mm_hist.append(tv.n)

                @blk.vector
                def _(v, oc=oc, nq=nq, i=i):
                    v.wait_ge(tv.h, wo_mm_hist[i])
                    if oc >= 2 and nq == 0:
                        ds_, dv_ = wo_dmat_hist[oc - 2]
                        v.wait_ge(ds_.h, dv_)
                    v.tensor_copy(wo_fe[:, oc % 2, nq * 512:(nq + 1) * 512],
                                  PS[:, i % 2, :]).then_inc(vv.h, 1)
                    vv.n += 1
                wo_ev_hist.append(vv.n)

            dsem = dts if oc % 2 == 0 else dt1

            @blk.sync
            def _(sy, oc=oc, dsem=dsem):
                sy.wait_ge(vv.h, wo_ev_hist[oc * NQ + NQ - 1])
                sy.dma_start_transpose(wo_tok[:, :, oc * 128:(oc + 1) * 128],
                                       wo_fe[:, oc % 2, :]).then_inc(dsem.h, 16)
            dsem.n += 16
            wo_dmat_hist.append((dsem, dsem.n))
        wo_t_done = [(dts, dts.n), (dt1, dt1.n)]

        res1_hist = []
        for tc in range(TC):
            @blk.vector
            def _(v, tc=tc):
                if tc == 0:
                    for ds_, dv_ in wo_t_done:
                        v.wait_ge(ds_.h, dv_)
                v.tensor_tensor(grs[:, :], wo_tok[:, tc, :], ga_bc[:, :], ALU.mult)
                v.tensor_tensor(x_tok[:, tc, :], x_tok[:, tc, :], grs[:, :], ALU.add
                                ).then_inc(vv.h, 1)
                vv.n += 1
            res1_hist.append(vv.n)
        res1_done = vv.n

        es3.close()
        esL2.close()

        es4 = ExitStack()
        S4 = lambda n, sh, dt: es4.enter_context(nc.sbuf_tensor(n, sh, dt))
        cb1_bc = S4("cb1_bc", [128, D], F32)
        sbv_bc = S4("sbv_bc", [128, D], F32)
        gb_bc = S4("gb_bc", [128, D], BF16)
        gbias_bc = S4("gbias_bc", [128, D], BF16)
        h2 = S4("h2", [128, D], F32)
        h2B = S4("h2B", [128, D], F32)
        tq = S4("tq", [128, D], F32)
        tqB = S4("tqB", [128, D], F32)
        dq = S4("dq", [128, D], F32)
        dqB = S4("dqB", [128, D], F32)
        sg1 = S4("sg1", [128, D], BF16)
        sg1B = S4("sg1B", [128, D], BF16)
        sg2 = S4("sg2", [128, D], BF16)
        sg2B = S4("sg2B", [128, D], BF16)
        q8s = S4("q8s", [128, D], I8)
        q8sB = S4("q8sB", [128, D], I8)
        mcolB = S4("mcolB", [128, 2], F32)
        q8tok = S4("q8tok", [128, 2, D], BF16)
        q8fe = S4("q8fe", [128, FC, QH], BF16)
        ffn_fe = S4("ffn_fe", [128, 2, QH], BF16)
        ffn_tok = S4("ffn_tok", [128, 8, D], BF16)
        grs2 = S4("grs2", [128, D], F32)

        @blk.gpsimd
        def _(g):
            g.wait_ge(vv.h, res1_done)
            g.dma_start(cb1_bc[:, :], scr[4:5, 0:D].partition_broadcast(128)).then_inc(lg.h, 16)
            g.dma_start(sbv_bc[:, :], scr[5:6, 0:D].partition_broadcast(128)).then_inc(lg.h, 16)
            g.dma_start(gb_bc[:, :], scr[7:8, 0:D].partition_broadcast(128)).then_inc(lg.h, 16)
            g.dma_start(gbias_bc[:, :], scr[8:9, 0:D].partition_broadcast(128)).then_inc(lg.h, 16)
        lg.n += 64
        p6_bc = lg.n

        @blk.vector
        def _(v):
            v.wait_ge(vv.h, res1_done)
            for tc in range(TC):
                v.tensor_scalar(h2[:, :], x_tok[:, tc, :], 0.0, 0.0, ALU.add, ALU.add,
                                accum_out=stx[:, tc:tc + 1])
                v.scalar_tensor_tensor(h2[:, :], x_tok[:, tc, :], 1.0, x_tok[:, tc, :],
                                       ALU.mult, ALU.mult, accum_out=stxx[:, tc:tc + 1])
            v.drain()
            v.tensor_scalar(mu3[:, :], stx[:, :], 1.0 / D, None, ALU.mult)
            v.drain()
            v.tensor_tensor(asb[:, :], mu3[:, :], mu3[:, :], ALU.mult)
            v.drain()
            v.scalar_tensor_tensor(rasb[:, :], stxx[:, :], 1.0 / D, asb[:, :],
                                   ALU.mult, ALU.subtract)
            v.drain()
            v.engine_nop().then_inc(vv.h, 1)
            vv.n += 1
        var3_done = vv.n

        @blk.scalar
        def _(s):
            s.wait_ge(vv.h, var3_done)
            s.activation(asb[:, :], rasb[:, :], AF.Ln, bias=sc1[:, 0:1])
            s.activation(r3c[:, :], asb[:, :], AF.Exp, scale=-0.5).then_inc(sv.h, 1)
            sv.n += 1
        r3_done = sv.n

        q8_dmat_hist = []
        q8i_hist = []
        ep_hist = []
        store_hist = []
        h2p, tqp, dqp = [h2, h2B], [tq, tqB], [dq, dqB]
        sg1p, sg2p, q8sp, mcolp = [sg1, sg1B], [sg2, sg2B], [q8s, q8sB], [mcol, mcolB]

        def emit_q_head(tc):
            P_ = tc % 2

            @blk.vector
            def _(v, tc=tc, P_=P_):
                if tc == 0:
                    v.wait_ge(sv.h, r3_done)
                    v.wait_ge(lg.h, p6_bc)
                if len(q8i_hist) >= 2:
                    v.wait_ge(sv.h, q8i_hist[-2])
                v.tensor_scalar(h2p[P_][:, :], x_tok[:, tc, :], mu3[:, tc:tc + 1],
                                r3c[:, tc:tc + 1], ALU.subtract, ALU.mult)
                v.tensor_tensor(h2p[P_][:, :], h2p[P_][:, :], cb1_bc[:, :], ALU.mult)
                v.tensor_tensor(h2p[P_][:, :], h2p[P_][:, :], sbv_bc[:, :], ALU.add)
                v.reduce_max(mcolp[P_][:, 0:1], h2p[P_][:, :], AX.X,
                             apply_absolute_value=True)
                v.drain()
                v.tensor_scalar_max(mcolp[P_][:, 0:1], mcolp[P_][:, 0:1], 1e-5)
                v.drain()
                v.reciprocal(mcolp[P_][:, 1:2], mcolp[P_][:, 0:1])
                v.drain()
                v.tensor_scalar(asb[:, tc:tc + 1], mcolp[P_][:, 1:2], 127.0, None,
                                ALU.mult)
                v.tensor_scalar(rasb[:, tc:tc + 1], mcolp[P_][:, 0:1], wsb[:, 0:1],
                                1.0 / 127.0, ALU.mult, ALU.mult)
                v.drain()
                v.engine_nop().then_inc(vv.h, 1)
                vv.n += 1
            h2_done = vv.n

            @blk.scalar
            def _(s, tc=tc, P_=P_, h2_done=h2_done):
                s.wait_ge(vv.h, h2_done)
                s.activation(tqp[P_][:, :], h2p[P_][:, :], AF.Copy,
                             scale=asb[:, tc:tc + 1])
                s.activation(q8sp[P_][:, :], h2p[P_][:, :], AF.Copy,
                             scale=asb[:, tc:tc + 1]).then_inc(sv.h, 1)
                sv.n += 1
            q8i_hist.append(sv.n)

        def emit_q_tail(tc, tci, q8i_done):
            P_ = tc % 2

            @blk.vector
            def _(v, P_=P_, q8i_done=q8i_done):
                v.wait_ge(sv.h, q8i_done)
                v.tensor_tensor(dqp[P_][:, :], tqp[P_][:, :], q8sp[P_][:, :],
                                ALU.subtract).then_inc(vv.h, 1)
                vv.n += 1
            d_done = vv.n

            @blk.scalar
            def _(s, P_=P_, d_done=d_done):
                s.wait_ge(vv.h, d_done)
                s.activation(sg1p[P_][:, :], dqp[P_][:, :], AF.Sign, bias=schalf[:, 0:1])
                s.activation(sg2p[P_][:, :], dqp[P_][:, :], AF.Sign, bias=schalf[:, 1:2]
                             ).then_inc(sv.h, 1)
                sv.n += 1
            sgn_done = sv.n

            @blk.vector
            def _(v, tc=tc, P_=P_, sgn_done=sgn_done):
                v.wait_ge(sv.h, sgn_done)
                if len(q8_dmat_hist) >= 2:
                    ds_, dv_ = q8_dmat_hist[-2]
                    v.wait_ge(ds_.h, dv_)
                v.tensor_tensor(sg1p[P_][:, :], sg1p[P_][:, :], sg2p[P_][:, :], ALU.add)
                v.scalar_tensor_tensor(q8tok[:, tc % 2, :], sg1p[P_][:, :], 0.5,
                                       q8sp[P_][:, :], ALU.mult, ALU.add
                                       ).then_inc(vv.h, 1)
                vv.n += 1
            q8_done = vv.n

            dsem = dts if tc % 2 == 0 else dt1

            @blk.sync
            def _(sy, tc=tc, tci=tci, q8_done=q8_done, dsem=dsem):
                sy.wait_ge(vv.h, q8_done)
                sy.dma_start_transpose(q8fe[:, :, tci * 128:(tci + 1) * 128],
                                       q8tok[:, tc % 2, :]).then_inc(dsem.h, 16)
            dsem.n += 16
            q8_dmat_hist.append((dsem, dsem.n))

        for rg in range(2):
            for tci in range(8):
                tc = rg * 8 + tci
                emit_q_head(tc)
                if tci >= 1:
                    emit_q_tail(tc - 1, tci - 1, q8i_hist[-2])
            emit_q_tail(rg * 8 + 7, 7, q8i_hist[-1])
            q8_all = [(dts, dts.n), (dt1, dt1.n)]

            f_mm_hist = []
            f_ev_hist = []
            f_dmat_hist = []
            for oc in range(FC):
                for n2 in range(2):
                    i = oc * 2 + n2

                    @blk.tensor
                    def _(t, rg=rg, oc=oc, n2=n2, i=i, q8_all=q8_all):
                        if i == 0:
                            for ds_, dv_ in q8_all:
                                t.wait_ge(ds_.h, dv_)
                        if i >= 2:
                            t.wait_ge(vv.h, f_ev_hist[i - 2])
                        for fc in range(FC):
                            ins = t.matmul(PS[:, i % 2, :], wt_ffn[:, fc, oc * 128:(oc + 1) * 128],
                                           q8fe[:, fc, n2 * 512:(n2 + 1) * 512],
                                           start=(fc == 0), stop=(fc == FC - 1))
                        ins.then_inc(tv.h, 1)
                        tv.n += 1
                    f_mm_hist.append(tv.n)

                    @blk.vector
                    def _(v, oc=oc, n2=n2, i=i):
                        v.wait_ge(tv.h, f_mm_hist[i])
                        if oc >= 2 and n2 == 0:
                            ds_, dv_ = f_dmat_hist[oc - 2]
                            v.wait_ge(ds_.h, dv_)
                        v.tensor_copy(ffn_fe[:, oc % 2, n2 * 512:(n2 + 1) * 512],
                                      PS[:, i % 2, :]).then_inc(vv.h, 1)
                        vv.n += 1
                    f_ev_hist.append(vv.n)

                dsem = dts if oc % 2 == 0 else dt1

                @blk.sync
                def _(sy, oc=oc, dsem=dsem):
                    sy.wait_ge(vv.h, f_ev_hist[oc * 2 + 1])
                    sy.dma_start_transpose(ffn_tok[:, :, oc * 128:(oc + 1) * 128],
                                           ffn_fe[:, oc % 2, :]).then_inc(dsem.h, 16)
                dsem.n += 16
                f_dmat_hist.append((dsem, dsem.n))
            f_t_done = [(dts, dts.n), (dt1, dt1.n)]

            for tci in range(8):
                tc = rg * 8 + tci

                @blk.vector
                def _(v, tc=tc, tci=tci, f_t_done=f_t_done):
                    if tci == 0:
                        for ds_, dv_ in f_t_done:
                            v.wait_ge(ds_.h, dv_)
                    v.tensor_scalar(sg1[:, :], ffn_tok[:, tci, :], rasb[:, tc:tc + 1],
                                    None, ALU.mult)
                    v.tensor_tensor(sg2[:, :], sg1[:, :], gb_bc[:, :], ALU.mult)
                    v.tensor_tensor(grs2[:, :], sg2[:, :], gbias_bc[:, :], ALU.add)
                    v.tensor_tensor(x_tok[:, tc, :], x_tok[:, tc, :], grs2[:, :], ALU.add
                                    ).then_inc(vv.h, 1)
                    vv.n += 1
                ep_hist.append(vv.n)

                @blk.sync
                def _(sy, tc=tc):
                    sy.wait_ge(vv.h, ep_hist[-1])
                    sy.dma_start(out_h[tc * 128:(tc + 1) * 128, :], x_tok[:, tc, :]
                                 ).then_inc(st.h, 16)
                    st.n += 16
                store_hist.append(st.n)

        @blk.sync
        def _(sy):
            sy.wait_ge(st.h, st.n)

        es4.close()
    return nc


_NC_CACHE = {}


def kernel(L0_attn_latents, L0_attn_ln_gamma, L0_attn_ln_beta, L0_attn_Wq,
           L0_attn_Wk, L0_attn_Wv, L0_attn_Wo, L0_ffn_packed_weight,
           L0_ffn_weight_scale, L0_ffn_bias, L0_c, L0_wmod):
    if "nc" not in _NC_CACHE:
        _NC_CACHE["nc"] = build_nc()
    nc = _NC_CACHE["nc"]
    x = np.ascontiguousarray(np.asarray(L0_attn_latents, dtype=np.float32))
    shared = {
        "ln_g": np.asarray(L0_attn_ln_gamma, np.float32).reshape(1, D),
        "ln_b": np.asarray(L0_attn_ln_beta, np.float32).reshape(1, D),
        "wq": np.ascontiguousarray(np.asarray(L0_attn_Wq, np.float32)),
        "wk": np.ascontiguousarray(np.asarray(L0_attn_Wk, np.float32)),
        "wv": np.ascontiguousarray(np.asarray(L0_attn_Wv, np.float32)),
        "wo": np.ascontiguousarray(np.asarray(L0_attn_Wo, np.float32)),
        "pk": np.ascontiguousarray(np.asarray(L0_ffn_packed_weight, np.int32
                                              ).reshape(D // 4, D)),
        "wsc": np.asarray(L0_ffn_weight_scale, np.float32).reshape(1, 1),
        "fbias": np.asarray(L0_ffn_bias, np.float32).reshape(1, D),
        "wmod": np.ascontiguousarray(np.asarray(L0_wmod, np.float32)),
    }
    in_maps = []
    for b in range(8):
        m = dict(shared)
        m["x"] = np.ascontiguousarray(x[b])
        m["cvec"] = np.ascontiguousarray(np.asarray(L0_c, np.float32)[b].reshape(1, D))
        in_maps.append(m)
    res = run_bass_kernel_spmd(nc, in_maps, core_ids=list(range(8)))
    return np.stack([res.results[b]["out"] for b in range(8)], axis=0)
